# version 05
# baseline (speedup 1.0000x reference)
import numpy as np
import ml_dtypes
from contextlib import ExitStack

import concourse.bass as bass
import concourse.mybir as mybir
from concourse.bass_utils import run_bass_kernel_spmd

F32 = mybir.dt.float32
BF16 = mybir.dt.bfloat16
AF = mybir.ActivationFunctionType
ALU = mybir.AluOpType

NCORES = 8
T_FULL = 16384
D = 1024
DI = 2048
DH = 4096
NST = 16
DTR = 64
TL = T_FULL // NCORES
DSL = DI // NCORES
BF = ml_dtypes.bfloat16


class Ctx:
    def __init__(self, nc, st):
        self.nc = nc
        self.st = st
        self.sems = {}
        self.prefix = ""

    class _Stage:
        def __init__(self, cx, prefix):
            self.cx = cx
            self.prefix = prefix

        def __enter__(self):
            self.old_st = self.cx.st
            self.old_pf = self.cx.prefix
            self.stack = ExitStack()
            self.stack.__enter__()
            self.cx.st = self.stack
            self.cx.prefix = self.old_pf + self.prefix
            return self

        def __exit__(self, *a):
            self.stack.__exit__(*a)
            self.cx.st = self.old_st
            self.cx.prefix = self.old_pf

    def stage(self, prefix):
        return Ctx._Stage(self, prefix)

    def sem(self, *names):
        for name in names:
            h = self.st.enter_context(self.nc.semaphore(name))
            self.sems[name] = [h, 0]

    def inc(self, inst, name, k=1):
        s = self.sems[name]
        inst.then_inc(s[0], k)
        s[1] += k
        return s[1]

    def n(self, name):
        return self.sems[name][1]

    def snapshot(self):
        return {k: v[1] for k, v in self.sems.items()}

    def wait(self, eng, name, val):
        if val > 0:
            eng.wait_ge(self.sems[name][0], val)

    def barrier_at(self, eng, snap):
        for nm, v in snap.items():
            if v > 0:
                eng.wait_ge(self.sems[nm][0], v)

    def sb(self, name, shape, dtype, stack=None, side=None):
        kw = {"side": side} if side else {}
        return (stack or self.st).enter_context(self.nc.sbuf_tensor(self.prefix + name, shape, dtype, **kw))

    def ps(self, name, shape, stack=None, dtype=F32):
        return (stack or self.st).enter_context(self.nc.psum_tensor(self.prefix + name, shape, dtype))


def _dtile(ap, k, c0, c1, tl):
    return ap[:, k * tl + c0:k * tl + c1]



def emit_ln_phase(nc, cx, ph, tl, xf, xnb, nw, ones1, onesr, eps_ap,
                  xnr_store=None, rms=True, gain_scale=None, gain_bias=None,
                  xnb2=None):
    NC4 = tl // 512
    xsq = cx.sb("lnxsq", [128, 8 * 512], F32, ph)
    stat = cx.sb("lnstat", [1, 8 * 512], F32, ph)
    xnrb = [cx.sb(f"lnxnrb{i}", [128, 512], BF16, ph) for i in range(2)] if rms else None
    p_s1 = cx.ps("lnp_s1", [1, 512], ph)
    p_s2 = cx.ps("lnp_s2", [1, 512], ph)
    p_rms = cx.ps("lnp_rms", [1, 512], ph)
    p_bc1 = cx.ps("lnp_bc1", [128, 512], ph)
    p_bc2 = cx.ps("lnp_bc2", [128, 512], ph)

    mu = stat[:, 0:512]
    ex2 = stat[:, 512:1024]
    msq = stat[:, 1024:1536]
    var = stat[:, 1536:2048]
    sd = stat[:, 2048:2560]
    rstd = stat[:, 2560:3072]
    mur = stat[:, 3072:3584]
    ms2 = stat[:, 3584:4096]

    snap = cx.snapshot()
    t0, s0, v0, so0 = cx.n("tf"), cx.n("sv"), cx.n("vv"), cx.n("so")
    NT = 27
    NS = 9 + (8 if xnb is not None else 0) + ((8 + 1) if rms else 0)
    NV = 22 + (10 if rms else 0)
    block_cm = nc.Block()
    block = block_cm.__enter__()

    @block.tensor
    def _(tensor):
        cx.barrier_at(tensor, snap)
        for c in range(NC4):
            bs, bv = s0 + c * NS, v0 + c * NV
            if c > 0:
                cx.wait(tensor, "vv", v0 + c * NV)
            for k in range(8):
                inst = tensor.matmul(p_s1[:, :], ones1[:, :], _dtile(xf, k, c * 512, (c + 1) * 512, tl),
                                     start=(k == 0), stop=(k == 7))
            cx.wait(tensor, "sv", bs + 8)
            for k in range(8):
                inst = tensor.matmul(p_s2[:, :], ones1[:, :], xsq[:, k * 512:(k + 1) * 512],
                                     start=(k == 0), stop=(k == 7))
            cx.inc(inst, "tf", 16)
            cx.wait(tensor, "vv", bv + 6)
            tensor.matmul(p_bc1[:, :], onesr[:, :], rstd, start=True, stop=True)
            inst = tensor.matmul(p_bc2[:, :], onesr[:, :], mur, start=True, stop=True)
            cx.inc(inst, "tf", 2 if rms else 11)
            if rms:
                cx.wait(tensor, "sv", bs + 9 + 16)
                for k in range(8):
                    inst = tensor.matmul(p_rms[:, :], ones1[:, :], xsq[:, k * 512:(k + 1) * 512],
                                         start=(k == 0), stop=(k == 7))
                cx.inc(inst, "tf", 8)
                cx.wait(tensor, "vv", bv + 24)
                inst = tensor.matmul(p_bc1[:, :], onesr[:, :], ms2, start=True, stop=True)
                cx.inc(inst, "tf", 1)


    @block.scalar
    def _(scalar):
        cx.barrier_at(scalar, snap)
        for c in range(NC4):
            bt, bv = t0 + c * NT, v0 + c * NV
            if c > 0:
                cx.wait(scalar, "tf", t0 + (c - 1) * NT + 16)
            for k in range(8):
                cx.inc(scalar.activation(xsq[:, k * 512:(k + 1) * 512],
                                         _dtile(xf, k, c * 512, (c + 1) * 512, tl), AF.Square), "sv", 1)
            cx.wait(scalar, "vv", bv + 4)
            cx.inc(scalar.activation(sd, var, AF.Sqrt, bias=eps_ap[0:1, :]), "sv", 1)
            cx.wait(scalar, "vv", bv + 22)
            for k in range(8):
                if xnb is not None and gain_scale is None:
                    cx.inc(scalar.activation(_dtile(xnb, k, c * 512, (c + 1) * 512, tl),
                                             _dtile(xf, k, c * 512, (c + 1) * 512, tl), AF.Copy), "sv", 1)
                if rms:
                    cx.inc(scalar.activation(xsq[:, k * 512:(k + 1) * 512],
                                             _dtile(xf, k, c * 512, (c + 1) * 512, tl), AF.Square), "sv", 1)
            if rms:
                cx.wait(scalar, "vv", bv + 23)
                cx.inc(scalar.activation(sd, ms2, AF.Sqrt, bias=eps_ap[0:1, :]), "sv", 1)

    @block.vector
    def _(vector):
        cx.barrier_at(vector, snap)
        for c in range(NC4):
            bt, bs = t0 + c * NT, s0 + c * NS
            cx.wait(vector, "tf", bt + 16)
            cx.inc(vector.tensor_scalar(mu, p_s1[:, :], 1.0 / D, None, ALU.mult), "vv", 1)
            cx.inc(vector.tensor_scalar(ex2, p_s2[:, :], 1.0 / D, None, ALU.mult), "vv", 1)
            cx.inc(vector.tensor_tensor(msq, mu, mu, ALU.mult), "vv", 1)
            cx.inc(vector.tensor_tensor(var, ex2, msq, ALU.subtract), "vv", 1)
            cx.wait(vector, "sv", bs + 9)
            cx.inc(vector.reciprocal(rstd, sd), "vv", 1)
            cx.inc(vector.tensor_tensor(mur, mu, rstd, ALU.mult), "vv", 1)
            cx.wait(vector, "tf", bt + 18)
            for k in range(8):
                sl = slice(k * tl + c * 512, k * tl + (c + 1) * 512)
                cx.inc(vector.tensor_tensor(xsq[:, k * 512:(k + 1) * 512], xf[:, sl], p_bc1[:, :], ALU.mult), "vv", 1)
                cx.inc(vector.tensor_tensor(xf[:, sl], xsq[:, k * 512:(k + 1) * 512], p_bc2[:, :], ALU.subtract), "vv", 1)
            if rms:
                cx.wait(vector, "tf", bt + 26)
                cx.inc(vector.tensor_scalar(ms2, p_rms[:, :], 1.0 / D, None, ALU.mult), "vv", 1)
                cx.wait(vector, "sv", bs + 26)
                cx.inc(vector.reciprocal(ms2, sd), "vv", 1)
                cx.wait(vector, "tf", bt + 27)
                for k in range(8):
                    idx = c * 8 + k
                    if idx >= 2:
                        cx.wait(vector, "so", so0 + 16 * (idx - 1))
                    sl = slice(k * tl + c * 512, k * tl + (c + 1) * 512)
                    cx.inc(vector.scalar_tensor_tensor(xnrb[idx % 2][:, :], xf[:, sl], nw[:, k:k + 1],
                                                       p_bc1[:, :], ALU.mult, ALU.mult), "vv", 1)


    if rms and xnr_store is not None:
        @block.sync
        def _(sync):
            cx.barrier_at(sync, snap)
            for c in range(NC4):
                for k in range(8):
                    idx = c * 8 + k
                    cx.wait(sync, "vv", v0 + c * NV + 24 + (k + 1))
                    xnr_store(sync, k, c, xnrb[idx % 2])

    block_cm.__exit__(None, None, None)
    return (t0 + NC4 * NT, s0 + cx_ns(NC4, NS), v0 + NC4 * NV)


def cx_ns(nc4, ns):
    return nc4 * ns


def _finalize(nc, cx):
    snap = cx.snapshot()
    with nc.Block() as bfin:
        @bfin.sync
        def _(sync):
            cx.barrier_at(sync, snap)



def build_stageA(tl=TL):
    nc = bass.Bass("TRN2", target_bir_lowering=False)
    io = {}
    io["xT"] = nc.dram_tensor("xT", [D, tl], F32, kind="ExternalInput").ap()
    io["normw"] = nc.dram_tensor("normw", [D, 1], F32, kind="ExternalInput").ap()
    for nm, k, m in [("w1T", D, DH), ("w2T", D, DH), ("h1T", DH, DH),
                     ("h2T", DH, DH), ("w3T", DH, D)]:
        io[nm] = nc.dram_tensor(nm, [k, m], F32, kind="ExternalInput").ap()
    for nm, d_ in [("w1s", DH), ("w2s", DH), ("h1s", DH), ("h2s", DH), ("w3s", D)]:
        io[nm] = nc.dram_tensor(nm, [d_, 1], F32, kind="ExternalInput").ap()
    io["xnr_o"] = nc.dram_tensor("xnr_o", [D, tl], BF16, kind="ExternalOutput").ap()
    io["xn_o"] = nc.dram_tensor("xn_o", [D, tl], BF16, kind="ExternalOutput").ap()
    io["ffn0_o"] = nc.dram_tensor("ffn0_o", [D, tl], F32, kind="ExternalOutput").ap()
    with ExitStack() as st:
        cx = Ctx(nc, st)
        cx.sem("ld", "wla", "wc", "tf", "sv", "vv", "so", "sa", "svc", "td", "tb", "tc", "cc")
        emit_stageA(nc, cx, io, tl)
    return nc


def emit_stageA(nc, cx, io, tl=TL):
    xT, normw = io["xT"], io["normw"]
    wT = {nm: io[nm] for nm in ["w1T", "w2T", "h1T", "h2T", "w3T"]}
    wS = {nm: io[nm] for nm in ["w1s", "w2s", "h1s", "h2s", "w3s"]}
    xnr_o, xn_o, ffn0_o = io["xnr_o"], io["xn_o"], io["ffn0_o"]
    with cx.stage("A_"):
        xnb = cx.sb("xnb", [128, 8 * tl], BF16)
        slotA = cx.sb("slotA", [128, 16 * tl], BF16)
        nw = cx.sb("nw", [128, 8], F32)
        ones1 = cx.sb("ones1", [128, 1], F32)
        onesr = cx.sb("onesr", [1, 128], F32)
        scales = cx.sb("scales", [128, 136], F32)
        onesc = nc.inline_tensor(np.ones((1, 128), np.float32), name="onesc").ap()
        onescol = nc.inline_tensor(np.ones((128, 1), np.float32), name="onescol").ap()
        consts = cx.sb("consts", [128, 4], F32)
        constvals = np.tile(np.array([[1e-5, 1e-6, 1.0, 0.0]], np.float32), (128, 1))
        constd = nc.inline_tensor(constvals, name="constd").ap()

        with nc.Block() as b0:
            @b0.sync
            def _(sync):
                _ncd = nc.allow_non_contiguous_dma(reason="small per-feature vectors")
                _ncd.__enter__()
                cx.inc(sync.dma_start(out=nw[:, :], in_=normw.rearrange("(a p) o -> p (a o)", p=128)), "ld", 16)
                cx.inc(sync.dma_start(out=onesr[:, :], in_=onesc), "ld", 16)
                cx.inc(sync.dma_start(out=ones1[:, :], in_=onescol), "ld", 16)
                for i, nm in enumerate(["w1s", "w2s", "h1s", "h2s"]):
                    cx.inc(sync.dma_start(out=scales[:, 32 * i:32 * (i + 1)],
                                          in_=wS[nm].rearrange("(m p) o -> p (m o)", p=128)), "ld", 16)
                cx.inc(sync.dma_start(out=scales[:, 128:136],
                                      in_=wS["w3s"].rearrange("(m p) o -> p (m o)", p=128)), "ld", 16)
                _ncd.__exit__(None, None, None)
                cx.inc(sync.dma_start(out=consts[:, :], in_=constd), "ld", 16)

        with ExitStack() as ph:
            xf = cx.sb("xf", [128, 8 * tl], F32, ph)
            with nc.Block() as bl:
                @bl.sync
                def _(sync):
                    cx.inc(sync.dma_start(out=xf[:, :].rearrange("p (a t) -> p a t", a=8),
                                          in_=xT.rearrange("(a p) t -> p a t", p=128)), "ld", 16)

            def xnr_store(sync, k, c, buf):
                cx.inc(sync.dma_start(
                    out=xnr_o.rearrange("(a p) t -> p a t", p=128)[:, k, c * 512:(c + 1) * 512],
                    in_=buf[:, :]), "so", 16)

            emit_ln_phase(nc, cx, ph, tl, xf, xnb, nw, ones1, onesr, consts[:, 0:1],
                          xnr_store=xnr_store, rms=True)

            with nc.Block() as b2:
                @b2.sync
                def _(sync):
                    cx.wait(sync, "sv", cx.n("sv"))
                    cx.inc(sync.dma_start(out=xn_o.rearrange("(a p) t -> p a t", p=128),
                                          in_=xnb[:, :].rearrange("p (a t) -> p a t", a=8)), "so", 16)

        _ffn_phase(nc, cx, None, tl, wT, scales, xnb, slotA, ffn0_o)
        _finalize(nc, cx)


def _ffn_phase(nc, cx, st, tl, wT, scales, xnb, slotA, ffn0_o):
    HALF = tl // 2

    with ExitStack() as ph:
        slotB = cx.sb("slotB", [128, 16 * tl], BF16, ph)
        wf = [cx.sb(f"wf{i}", [128, 2048], F32, ph) for i in range(2)]
        wb = [cx.sb(f"wb{i}", [128, 2048], BF16, ph) for i in range(2)]
        t1b = [cx.sb(f"t1b{i}", [128, 1024], BF16, ph) for i in range(2)]
        t2b = [cx.sb(f"t2b{i}", [128, 1024], BF16, ph) for i in range(2)]
        f3 = [cx.sb(f"f3{i}", [128, 1024], F32, ph) for i in range(2)]
        pp = [cx.ps(f"pp{i}", [128, 512], ph) for i in range(8)]

        jobs = []
        groups = []
        for half in range(2):
            for m in range(32):
                jobs.append(("L1", half, m, 0, wT["w1T"][:, m * 128:(m + 1) * 128], 8))
                jobs.append(("L1", half, m, 1, wT["w2T"][:, m * 128:(m + 1) * 128], 8))
                groups.append(("L1", half, m))
            for m in range(32):
                for s2 in range(2):
                    jobs.append(("H1", half, m, s2, wT["h1T"][s2 * 2048:(s2 + 1) * 2048, m * 128:(m + 1) * 128], 16))
                groups.append(("H1", half, m))
            for m in range(32):
                for s2 in range(2):
                    jobs.append(("H2", half, m, s2, wT["h2T"][s2 * 2048:(s2 + 1) * 2048, m * 128:(m + 1) * 128], 16))
                groups.append(("H2", half, m))
            for m in range(8):
                for s2 in range(2):
                    jobs.append(("W3", half, m, s2, wT["w3T"][s2 * 2048:(s2 + 1) * 2048, m * 128:(m + 1) * 128], 16))
                groups.append(("W3", half, m))
        NJ, NG = len(jobs), len(groups)
        GPH = NG // 2

        def sv_after(g):
            if g < 0:
                return 0
            n = 0
            for gg in range(g + 1):
                tag = groups[gg][0]
                n += 4 if tag == "L1" else 2
            return n

        def hm_after(g):
            if g < 0:
                return 0
            return 2 * sum(1 for gg in range(g + 1) if groups[gg][0] == "L1")

        snap = cx.snapshot()
        tf0, sv0, vv0, wc0, so0, wl0 = (cx.n("tf"), cx.n("sv"), cx.n("vv"),
                                        cx.n("wc"), cx.n("so"), cx.n("wla"))

        block_cm = nc.Block()
        block = block_cm.__enter__()

        @block.sync
        def _(sync):
            cx.barrier_at(sync, snap)
            store_seq = 0
            for j, (tag, half, m, s2, ap, nk) in enumerate(jobs):
                if j >= 2:
                    cx.wait(sync, "wc", wc0 + j - 1)
                cx.inc(sync.dma_start(out=wf[j % 2][:, 0:nk * 128].rearrange("p (a q) -> p a q", q=128),
                                      in_=ap.rearrange("(a p) q -> p a q", p=128)), "wla", 16)
                if tag == "W3" and s2 == 1:
                    g = half * GPH + 96 + m
                    cx.wait(sync, "sv", sv0 + sv_after(g))
                    cx.inc(sync.dma_start(
                        out=ffn0_o.rearrange("(a p) t -> p a t", p=128)[:, m, half * HALF:(half + 1) * HALF],
                        in_=f3[m % 2][:, 0:HALF]), "so", 16)
                    store_seq += 1

        @block.vector
        def _(vector):
            cx.barrier_at(vector, snap)
            for j, (tag, half, m, s2, ap, nk) in enumerate(jobs):
                cx.wait(vector, "wla", wl0 + 16 * (j + 1))
                if j >= 2:
                    cx.wait(vector, "tf", tf0 + j - 1)
                cx.inc(vector.tensor_copy(wb[j % 2][:, 0:nk * 128], wf[j % 2][:, 0:nk * 128]), "wc", 1)
                if s2 == 1:
                    gp = j // 2 - 1
                    if gp >= 0 and groups[gp][0] == "L1":
                        _, hh, mm = groups[gp]
                        par = gp % 2
                        cx.wait(vector, "sv", sv0 + sv_after(gp))
                        for c2 in range(2):
                            cx.inc(vector.tensor_tensor(
                                slotA[:, mm * HALF + c2 * 512: mm * HALF + (c2 + 1) * 512],
                                t1b[par][:, c2 * 512:(c2 + 1) * 512],
                                t2b[par][:, c2 * 512:(c2 + 1) * 512], ALU.mult), "vv", 1)

        @block.tensor
        def _(tensor):
            cx.barrier_at(tensor, snap)
            for g, (tag, half, m) in enumerate(groups):
                par = g % 2
                if g >= 2:
                    cx.wait(tensor, "sv", sv0 + sv_after(g - 2))
                if tag == "H1" and m == 0:
                    cx.wait(tensor, "vv", vv0 + hm_after(half * GPH + 31))
                if tag == "H2" and m == 0:
                    cx.wait(tensor, "sv", sv0 + sv_after(half * GPH + 63))
                if tag == "W3" and m == 0:
                    cx.wait(tensor, "sv", sv0 + sv_after(half * GPH + 95))
                for s2 in range(2):
                    j = 2 * g + s2
                    cx.wait(tensor, "wc", wc0 + j + 1)
                    nk = jobs[j][5]
                    if tag == "L1":
                        for k in range(8):
                            for c2 in range(2):
                                inst = tensor.matmul(
                                    pp[par * 4 + s2 * 2 + c2][:, :],
                                    wb[j % 2][:, k * 128:(k + 1) * 128],
                                    xnb[:, k * tl + half * HALF + c2 * 512:
                                        k * tl + half * HALF + (c2 + 1) * 512],
                                    start=(k == 0), stop=(k == 7))
                    else:
                        src = {"H1": slotA, "H2": slotB, "W3": slotA}[tag]
                        for k in range(16):
                            kk = s2 * 16 + k
                            for c2 in range(2):
                                inst = tensor.matmul(
                                    pp[par * 4 + c2][:, :],
                                    wb[j % 2][:, k * 128:(k + 1) * 128],
                                    src[:, kk * HALF + c2 * 512:kk * HALF + (c2 + 1) * 512],
                                    start=(kk == 0), stop=(kk == 31))
                    cx.inc(inst, "tf", 1)

        @block.scalar
        def _(scalar):
            cx.barrier_at(scalar, snap)
            store_seq = 0
            for g, (tag, half, m) in enumerate(groups):
                par = g % 2
                cx.wait(scalar, "tf", tf0 + 2 * (g + 1))
                if tag == "L1":
                    if g >= 2 and groups[g - 2][0] == "L1":
                        cx.wait(scalar, "vv", vv0 + hm_after(g - 2))
                    for c2 in range(2):
                        cx.inc(scalar.activation(t1b[par][:, c2 * 512:(c2 + 1) * 512],
                                                 pp[par * 4 + c2][:, :], AF.Silu,
                                                 scale=scales[:, m:m + 1]), "sv", 1)
                        cx.inc(scalar.activation(t2b[par][:, c2 * 512:(c2 + 1) * 512],
                                                 pp[par * 4 + 2 + c2][:, :], AF.Copy,
                                                 scale=scales[:, 32 + m:32 + m + 1]), "sv", 1)
                elif tag in ("H1", "H2"):
                    dst = slotB if tag == "H1" else slotA
                    soff = 64 if tag == "H1" else 96
                    for c2 in range(2):
                        cx.inc(scalar.activation(dst[:, m * HALF + c2 * 512:m * HALF + (c2 + 1) * 512],
                                                 pp[par * 4 + c2][:, :], AF.Gelu,
                                                 scale=scales[:, soff + m:soff + m + 1]), "sv", 1)
                else:
                    if store_seq >= 2:
                        cx.wait(scalar, "so", so0 + 16 * (store_seq - 1) + 0)
                    for c2 in range(2):
                        cx.inc(scalar.activation(f3[m % 2][:, c2 * 512:(c2 + 1) * 512],
                                                 pp[par * 4 + c2][:, :], AF.Copy,
                                                 scale=scales[:, 128 + m:128 + m + 1]), "sv", 1)
                    store_seq += 1

        block_cm.__exit__(None, None, None)



def build_stageB(t_full=T_FULL):
    nc = bass.Bass("TRN2", target_bir_lowering=False)
    io = {}
    io["xnr"] = nc.dram_tensor("xnr", [D, t_full], BF16, kind="ExternalInput").ap()
    io["winT"] = nc.dram_tensor("winT", [D, 512], F32, kind="ExternalInput").ap()
    io["convw"] = nc.dram_tensor("convw", [256, 4], F32, kind="ExternalInput").ap()
    io["convb"] = nc.dram_tensor("convb", [256, 1], F32, kind="ExternalInput").ap()
    io["wx"] = nc.dram_tensor("wx", [256, 96], F32, kind="ExternalInput").ap()
    io["u_o"] = nc.dram_tensor("u_o", [256, t_full], BF16, kind="ExternalOutput").ap()
    io["sres_o"] = nc.dram_tensor("sres_o", [256, t_full], BF16, kind="ExternalOutput").ap()
    io["xdbcp_o"] = nc.dram_tensor("xdbcp_o", [96, t_full], F32, kind="ExternalOutput").ap()
    with ExitStack() as st:
        cx = Ctx(nc, st)
        cx.sem("ld", "wla", "wc", "tf", "sv", "vv", "so", "sa", "svc", "td", "tb", "tc", "cc")
        emit_stageB(nc, cx, io, t_full)
    return nc


def emit_stageB(nc, cx, io, t_full=T_FULL, xnr_gathered=False):
    xnr, winT, convw, convb, wx = io["xnr"], io["winT"], io["convw"], io["convb"], io["wx"]
    u_o, sres_o, xdbcp_o = io["u_o"], io["sres_o"], io["xdbcp_o"]
    NCH = t_full // 1024
    with cx.stage("B_"):
        ld0 = cx.n("ld")
        winf = cx.sb("winf", [128, 4096], F32)
        winb = cx.sb("winb", [128, 4096], BF16)
        wxf = cx.sb("wxf", [128, 192], F32)
        wxb = cx.sb("wxb", [128, 192], BF16)
        cw = cx.sb("cw", [128, 8], F32)
        cb = cx.sb("cb", [128, 2], F32)
        xr = [cx.sb(f"xr{i}", [128, 8 * 1024], BF16) for i in range(2)]
        uraw = [cx.sb(f"uraw{g}", [128, 1027], F32) for g in range(2)]
        cacc = [cx.sb(f"cacc{g}", [128, 2 * 1024], F32) for g in range(2)]
        ub = [cx.sb(f"ub{g}", [128, 1024], BF16) for g in range(2)]
        sres = [cx.sb(f"sres{g}", [128, 1024], BF16) for g in range(2)]
        xbf = cx.sb("xbf", [96, 1024], F32)
        pp = [cx.ps(f"pp{i}", [128, 512]) for i in range(8)]

        with nc.Block() as b0:
            @b0.sync
            def _(sync):
                _n = nc.allow_non_contiguous_dma(reason="tiny per-channel vectors")
                _n.__enter__()
                cx.inc(sync.dma_start(out=winf[:, :].rearrange("p (a q) -> p a q", q=512),
                                      in_=winT.rearrange("(a p) q -> p a q", p=128)), "ld", 16)
                cx.inc(sync.dma_start(out=wxf[:, :].rearrange("p (a q) -> p a q", q=96),
                                      in_=wx.rearrange("(a p) q -> p a q", p=128)), "ld", 16)
                cx.inc(sync.dma_start(out=cw[:, :].rearrange("p (a q) -> p a q", q=4),
                                      in_=convw.rearrange("(a p) q -> p a q", p=128)), "ld", 16)
                cx.inc(sync.dma_start(out=cb[:, :].rearrange("p (a q) -> p a q", q=1),
                                      in_=convb.rearrange("(a p) q -> p a q", p=128)), "ld", 16)
                _n.__exit__(None, None, None)

        with nc.Block() as b1:
            @b1.vector
            def _(vector):
                cx.wait(vector, "ld", ld0 + 16 * 4)
                cx.inc(vector.tensor_copy(winb[:, :], winf[:, :]), "wc", 1)
                cx.inc(vector.tensor_copy(wxb[:, :], wxf[:, :]), "wc", 1)
                cx.inc(vector.memset(uraw[0][:, 0:3], 0.0), "wc", 1)
                cx.inc(vector.memset(uraw[1][:, 0:3], 0.0), "wc", 1)

        snap = cx.snapshot()
        NT = 64 + 4
        NS = 8 + 2 + 2
        NV = 10

        with nc.Block() as blk:
            @blk.sync
            def _(sync):
                cx.barrier_at(sync, snap)
                for c in range(NCH):
                    if c >= 2:
                        cx.wait(sync, "tf", snap["tf"] + (c - 1) * NT)
                    if xnr_gathered:
                        r, lc = (c * 1024) // TL, (c * 1024) % TL
                        src = xnr[r * D:(r + 1) * D, :].rearrange("(a p) t -> p a t", p=128)[:, :, lc:lc + 1024]
                    else:
                        src = xnr.rearrange("(a p) t -> p a t", p=128)[:, :, c * 1024:(c + 1) * 1024]
                    cx.inc(sync.dma_start(out=xr[c % 2][:, :].rearrange("p (a t) -> p a t", a=8),
                                          in_=src), "wla", 16)
                    cx.wait(sync, "sv", snap["sv"] + c * NS + NS)
                    for g in range(2):
                        cx.inc(sync.dma_start(out=u_o[g * 128:(g + 1) * 128, c * 1024:(c + 1) * 1024],
                                              in_=ub[g][:, :]), "so", 16)
                        cx.inc(sync.dma_start(out=sres_o[g * 128:(g + 1) * 128, c * 1024:(c + 1) * 1024],
                                              in_=sres[g][:, :]), "so", 16)
                    cx.inc(sync.dma_start(out=xdbcp_o[:, c * 1024:(c + 1) * 1024], in_=xbf[:, :]), "so", 16)

            @blk.tensor
            def _(tensor):
                cx.barrier_at(tensor, snap)
                for c in range(NCH):
                    cx.wait(tensor, "wla", snap["wla"] + 16 * (c + 1))
                    if c >= 1:
                        cx.wait(tensor, "sv", snap["sv"] + c * NS)
                    inst = None
                    for mt in range(4):
                        for c2 in range(2):
                            for k in range(8):
                                inst = tensor.matmul(pp[mt * 2 + c2][:, :],
                                                     winb[:, k * 512 + mt * 128:k * 512 + (mt + 1) * 128],
                                                     xr[c % 2][:, k * 1024 + c2 * 512:k * 1024 + (c2 + 1) * 512],
                                                     start=(k == 0), stop=(k == 7))
                    cx.inc(inst, "tf", 64)
                    cx.wait(tensor, "sv", snap["sv"] + c * NS + 10)
                    for c2 in range(2):
                        for g in range(2):
                            inst = tensor.matmul(pp[c2][0:96, :], wxb[:, g * 96:(g + 1) * 96],
                                                 ub[g][:, c2 * 512:(c2 + 1) * 512],
                                                 start=(g == 0), stop=(g == 1))
                    cx.inc(inst, "tf", 4)

            @blk.scalar
            def _(scalar):
                cx.barrier_at(scalar, snap)
                for c in range(NCH):
                    bt = snap["tf"] + c * NT
                    bv = snap["vv"] + c * NV
                    cx.wait(scalar, "tf", bt + 64)
                    if c >= 1:
                        cx.wait(scalar, "vv", bv)
                    for g in range(2):
                        for c2 in range(2):
                            cx.inc(scalar.activation(uraw[g][:, 3 + c2 * 512:3 + (c2 + 1) * 512],
                                                     pp[g * 2 + c2][:, :], AF.Copy), "sv", 1)
                    for g in range(2):
                        for c2 in range(2):
                            cx.inc(scalar.activation(sres[g][:, c2 * 512:(c2 + 1) * 512],
                                                     pp[4 + g * 2 + c2][:, :], AF.Silu), "sv", 1)
                    cx.wait(scalar, "vv", bv + 10)
                    for g in range(2):
                        cx.inc(scalar.activation(ub[g][:, :], cacc[g][:, 1024:2048],
                                                 AF.Silu, bias=cb[:, g:g + 1]), "sv", 1)
                    cx.wait(scalar, "tf", bt + 68)
                    for c2 in range(2):
                        cx.inc(scalar.activation(xbf[:, c2 * 512:(c2 + 1) * 512],
                                                 pp[c2][0:96, :], AF.Copy), "sv", 1)

            @blk.vector
            def _(vector):
                cx.barrier_at(vector, snap)
                for c in range(NCH):
                    bs = snap["sv"] + c * NS
                    vector.wait_ge(cx.sems["sv"][0], bs + 4)
                    for g in range(2):
                        a0 = cacc[g][:, 0:1024]
                        a1 = cacc[g][:, 1024:2048]
                        cx.inc(vector.tensor_scalar(a0, uraw[g][:, 0:1024], cw[:, g * 4:g * 4 + 1],
                                                    None, ALU.mult), "vv", 1)
                        cx.inc(vector.scalar_tensor_tensor(a1, uraw[g][:, 1:1025], cw[:, g * 4 + 1:g * 4 + 2],
                                                           a0, ALU.mult, ALU.add), "vv", 1)
                        cx.inc(vector.scalar_tensor_tensor(a0, uraw[g][:, 2:1026], cw[:, g * 4 + 2:g * 4 + 3],
                                                           a1, ALU.mult, ALU.add), "vv", 1)
                        cx.inc(vector.scalar_tensor_tensor(a1, uraw[g][:, 3:1027], cw[:, g * 4 + 3:g * 4 + 4],
                                                           a0, ALU.mult, ALU.add), "vv", 1)
                    for g in range(2):
                        cx.inc(vector.tensor_copy(uraw[g][:, 0:3], uraw[g][:, 1024:1027]), "vv", 1)

        _finalize(nc, cx)



def build_stageC(t_full=T_FULL, nsrc=8):
    nc = bass.Bass("TRN2", target_bir_lowering=False)
    io = {}
    io["xp"] = nc.dram_tensor("xp", [nsrc, 96, t_full], F32, kind="ExternalInput").ap()
    io["u_i"] = nc.dram_tensor("u_i", [256, t_full], BF16, kind="ExternalInput").ap()
    io["sres_i"] = nc.dram_tensor("sres_i", [256, t_full], BF16, kind="ExternalInput").ap()
    io["wdt"] = nc.dram_tensor("wdt", [DTR, 256], F32, kind="ExternalInput").ap()
    io["bdt"] = nc.dram_tensor("bdt", [256, 1], F32, kind="ExternalInput").ap()
    io["alog"] = nc.dram_tensor("alog", [256, NST], F32, kind="ExternalInput").ap()
    io["dsl"] = nc.dram_tensor("dsl", [256, 1], F32, kind="ExternalInput").ap()
    io["y_o"] = nc.dram_tensor("y_o", [256, t_full], BF16, kind="ExternalOutput").ap()
    with ExitStack() as st:
        cx = Ctx(nc, st)
        cx.sem("ld", "wla", "wc", "tf", "sv", "vv", "so", "sa", "svc", "td", "tb", "tc", "cc")
        emit_stageC(nc, cx, io, t_full, nsrc)
    return nc


def emit_stageC(nc, cx, io, t_full=T_FULL, nsrc=8):
    u_i, sres_i = io["u_i"], io["sres_i"]
    wdt, bdt, alog, dsl = io["wdt"], io["bdt"], io["alog"], io["dsl"]
    y_o = io["y_o"]
    xp = io["xp"]
    summed = io.get("xp_summed", False)
    NC8 = t_full // 2048

    ohB = np.zeros((NST, 32, 128), np.float32)
    ohC = np.zeros((NST, 32, 128), np.float32)
    for n in range(NST):
        ohB[n, n, :] = 1.0
        ohC[n, 16 + n, :] = 1.0

    with cx.stage("C_"):
        xb = cx.sb("xb", [96, 2048], F32)
        xp_sb = None if summed else cx.sb("xp_sb", [96, nsrc * 2048], F32)
        dtb = cx.sb("dtb", [DTR, 2048], BF16)
        bc_sb = cx.sb("bc_sb", [32, 2048], F32)
        u_sb = [cx.sb(f"u_sb{g}", [128, 2048], BF16) for g in range(2)]
        sres_sb = [cx.sb(f"sres_sb{g}", [128, 2048], BF16) for g in range(2)]
        delta_sb = [cx.sb(f"delta_sb{i}", [128, 512], F32) for i in range(2)]
        du_sb = [cx.sb(f"du_sb{i}", [128, 512], F32) for i in range(2)]
        sp1 = [cx.sb(f"sp1_{i}", [128, 512], F32) for i in range(2)]
        spe = [cx.sb(f"spe_{i}", [128, 512], F32) for i in range(2)]
        sp2 = [cx.sb(f"sp2_{i}", [128, 512], F32) for i in range(2)]
        a_sb = [cx.sb(f"a_sb{i}", [128, 512], F32) for i in range(2)]
        b_sb = [cx.sb(f"b_sb{i}", [128, 512], F32) for i in range(2)]
        H = cx.sb("H", [128, NST * 512], F32)
        C_all = cx.sb("C_all", [128, NST * 512], BF16)
        carry = cx.sb("carry", [128, 32], F32)
        yt_sb = cx.sb("yt_sb", [128, 512], F32)
        yf_sb = [cx.sb(f"yf_sb{i}", [128, 512], BF16) for i in range(2)]
        Anegr = cx.sb("Anegr", [128, 32], F32)
        Aneg = cx.sb("Aneg", [128, 32], F32)
        wdtf = cx.sb("wdtf", [DTR, 256], F32)
        wdtb = cx.sb("wdtb", [DTR, 256], BF16)
        dcol = cx.sb("dcol", [128, 2], F32)
        bdtc = cx.sb("bdtc", [128, 2], F32)
        oh_sb = cx.sb("oh_sb", [32, 2 * NST * 128], F32)
        ohd = nc.inline_tensor(
            np.concatenate([ohB.transpose(1, 0, 2).reshape(32, NST * 128),
                            ohC.transpose(1, 0, 2).reshape(32, NST * 128)], axis=1),
            name="C_ohd").ap()

        p_d = [cx.ps(f"p_d{i}", [128, 512]) for i in range(2)]
        p_bc = [cx.ps(f"p_bc{i}", [128, 512]) for i in range(2)]
        p_cc = [cx.ps(f"p_cc{i}", [128, 512]) for i in range(2)]

        NPRE = 2 if summed else 9
        NLD = 5 if summed else 12
        NV = NPRE + 8 * 42

        ld0 = cx.n("ld")

        with nc.Block() as b0:
            @b0.sync
            def _(sync):
                _n = nc.allow_non_contiguous_dma(reason="tiny vectors")
                _n.__enter__()
                cx.inc(sync.dma_start(out=wdtf[:, :], in_=wdt), "ld", 16)
                cx.inc(sync.dma_start(out=bdtc[:, :].rearrange("p (g o) -> p g o", o=1),
                                      in_=bdt.rearrange("(g p) o -> p g o", p=128)), "ld", 16)
                cx.inc(sync.dma_start(out=dcol[:, :].rearrange("p (g o) -> p g o", o=1),
                                      in_=dsl.rearrange("(g p) o -> p g o", p=128)), "ld", 16)
                cx.inc(sync.dma_start(out=Anegr[:, :].rearrange("p (g n) -> p g n", n=NST),
                                      in_=alog.rearrange("(g p) n -> p g n", p=128)), "ld", 16)
                cx.inc(sync.dma_start(out=oh_sb[:, :], in_=ohd), "ld", 16)
                _n.__exit__(None, None, None)

        wc0 = cx.n("wc")
        with nc.Block() as b1:
            @b1.scalar
            def _(scalar):
                cx.wait(scalar, "ld", ld0 + 16 * 5)
                cx.inc(scalar.activation(Aneg[:, :], Anegr[:, :], AF.Exp), "wc", 1)
            @b1.vector
            def _(vector):
                cx.wait(vector, "ld", ld0 + 16 * 5)
                cx.wait(vector, "wc", wc0 + 1)
                cx.inc(vector.tensor_scalar(Aneg[:, :], Aneg[:, :], -1.0, None, ALU.mult), "wc", 1)
                cx.inc(vector.tensor_copy(wdtb[:, :], wdtf[:, :]), "wc", 1)
                cx.inc(vector.memset(carry[:, :], 0.0), "wc", 1)

        snap = cx.snapshot()
        so0 = snap["so"]
        vv0 = snap["vv"]
        sv0 = snap["sv"]
        wl0 = snap["wla"]

        def blockbase(BL):
            return vv0 + (BL // 8) * NV + NPRE + (BL % 8) * 42

        block_cm = nc.Block()
        block = block_cm.__enter__()

        @block.sync
        def _(sync):
            cx.barrier_at(sync, snap)
            for c8 in range(NC8):
                if c8 >= 1:
                    cx.wait(sync, "vv", vv0 + c8 * NV)
                if summed:
                    cx.inc(sync.dma_start(out=xb[:, :], in_=xp[:, c8 * 2048:(c8 + 1) * 2048]), "wla", 16)
                else:
                    for s_ in range(nsrc):
                        cx.inc(sync.dma_start(out=xp_sb[:, s_ * 2048:(s_ + 1) * 2048],
                                              in_=xp[s_, :, c8 * 2048:(c8 + 1) * 2048]), "wla", 16)
                for g in range(2):
                    cx.inc(sync.dma_start(out=u_sb[g][:, :],
                                          in_=u_i[g * 128:(g + 1) * 128, c8 * 2048:(c8 + 1) * 2048]), "wla", 16)
                    cx.inc(sync.dma_start(out=sres_sb[g][:, :],
                                          in_=sres_i[g * 128:(g + 1) * 128, c8 * 2048:(c8 + 1) * 2048]), "wla", 16)
                for c2 in range(4):
                    for g in range(2):
                        BL = c8 * 8 + c2 * 2 + g
                        cx.wait(sync, "vv", blockbase(BL) + 42)
                        cx.inc(sync.dma_start(out=y_o[g * 128:(g + 1) * 128,
                                                      c8 * 2048 + c2 * 512:c8 * 2048 + (c2 + 1) * 512],
                                              in_=yf_sb[BL % 2][:, :]), "so", 16)

        @block.vector
        def _(vector):
            cx.barrier_at(vector, snap)
            for c8 in range(NC8):
                cx.wait(vector, "wla", wl0 + 16 * (c8 * NLD + NLD))
                cx.wait(vector, "tb", c8 * 128)
                cx.wait(vector, "tc", c8 * 64)
                cx.wait(vector, "td", c8 * 8)
                if not summed:
                    for s_ in range(4):
                        cx.inc(vector.tensor_tensor(xp_sb[:, s_ * 2048:(s_ + 1) * 2048],
                                                    xp_sb[:, s_ * 2048:(s_ + 1) * 2048],
                                                    xp_sb[:, (s_ + 4) * 2048:(s_ + 5) * 2048], ALU.add), "vv", 1)
                    for s_ in range(2):
                        cx.inc(vector.tensor_tensor(xp_sb[:, s_ * 2048:(s_ + 1) * 2048],
                                                    xp_sb[:, s_ * 2048:(s_ + 1) * 2048],
                                                    xp_sb[:, (s_ + 2) * 2048:(s_ + 3) * 2048], ALU.add), "vv", 1)
                    cx.inc(vector.tensor_tensor(xb[:, :], xp_sb[:, 0:2048],
                                                xp_sb[:, 2048:4096], ALU.add), "vv", 1)
                cx.inc(vector.tensor_copy(dtb[:, :], xb[0:DTR, :]), "vv", 1)
                cx.inc(vector.tensor_copy(bc_sb[:, :], xb[64:96, :]), "vv", 1)
                for c2 in range(4):
                    for g in range(2):
                        BL = c8 * 8 + c2 * 2 + g
                        bb = blockbase(BL)
                        cx.wait(vector, "sv", sv0 + 4 * (BL + 1))
                        if BL >= 2:
                            cx.wait(vector, "sa", 16 * (BL - 1))
                        cx.inc(vector.tensor_tensor(delta_sb[BL % 2][:, :], sp1[BL % 2][:, :],
                                                    sp2[BL % 2][:, :], ALU.add), "vv", 1)
                        cx.inc(vector.tensor_tensor(du_sb[BL % 2][:, :], delta_sb[BL % 2][:, :],
                                                    u_sb[g][:, c2 * 512:(c2 + 1) * 512], ALU.mult), "vv", 1)
                        for n in range(NST):
                            F_ = BL * 16 + n
                            cx.wait(vector, "tb", F_ + 1)
                            cx.wait(vector, "sa", BL * 16 + n + 1)
                            cx.inc(vector.tensor_tensor(b_sb[n % 2][:, :], du_sb[BL % 2][:, :],
                                                        p_bc[n % 2][:, :], ALU.mult), "vv", 1)
                            cx.inc(vector.tensor_tensor_scan(H[:, n * 512:(n + 1) * 512],
                                                             a_sb[n % 2][:, :], b_sb[n % 2][:, :],
                                                             carry[:, g * NST + n:g * NST + n + 1],
                                                             ALU.mult, ALU.add), "vv", 1)
                        cx.inc(vector.tensor_copy(
                            carry[:, g * NST:(g + 1) * NST],
                            H[:, :].rearrange("p (n t) -> p n t", t=512)[:, :, 511]), "vv", 1)
                        cx.wait(vector, "svc", (c8 * 4 + c2 + 1) * 16)
                        cx.inc(vector.tensor_tensor(H[:, :], H[:, :], C_all[:, :], ALU.mult), "vv", 1)
                        w = NST * 512
                        while w > 512:
                            cx.inc(vector.tensor_tensor(H[:, 0:w // 2], H[:, 0:w // 2],
                                                        H[:, w // 2:w], ALU.add), "vv", 1)
                            w //= 2
                        cx.inc(vector.scalar_tensor_tensor(yt_sb[:, :], u_sb[g][:, c2 * 512:(c2 + 1) * 512],
                                                           dcol[:, g:g + 1], H[:, 0:512],
                                                           ALU.mult, ALU.add), "vv", 1)
                        if BL >= 2:
                            cx.wait(vector, "so", so0 + 16 * (BL - 1))
                        cx.inc(vector.tensor_tensor(yf_sb[BL % 2][:, :], yt_sb[:, :],
                                                    sres_sb[g][:, c2 * 512:(c2 + 1) * 512], ALU.mult), "vv", 1)

        @block.scalar
        def _(scalar):
            cx.barrier_at(scalar, snap)
            for c8 in range(NC8):
                for c2 in range(4):
                    CE0 = (c8 * 4 + c2) * 16
                    if c8 + c2 > 0:
                        BLprev = c8 * 8 + c2 * 2 - 1
                        cx.wait(scalar, "vv", blockbase(BLprev) + 36)
                    for n in range(NST):
                        cx.wait(scalar, "tc", CE0 + n + 1)
                        cx.inc(scalar.activation(C_all[:, n * 512:(n + 1) * 512],
                                                 p_cc[n % 2][:, :], AF.Copy), "svc", 1)
                    for g in range(2):
                        BL = c8 * 8 + c2 * 2 + g
                        bb = blockbase(BL)
                        if BL >= 2:
                            cx.wait(scalar, "vv", blockbase(BL - 2) + 1)
                        cx.wait(scalar, "td", BL + 1)
                        cx.inc(scalar.activation(sp1[BL % 2][:, :], p_d[BL % 2][:, :], AF.Abs,
                                                 bias=bdtc[:, g:g + 1]), "sv", 1)
                        cx.inc(scalar.activation(spe[BL % 2][:, :], sp1[BL % 2][:, :], AF.Exp,
                                                 scale=-1.0), "sv", 1)
                        cx.inc(scalar.activation(sp1[BL % 2][:, :], spe[BL % 2][:, :], AF.Ln,
                                                 bias=1.0), "sv", 1)
                        cx.inc(scalar.activation(sp2[BL % 2][:, :], p_d[BL % 2][:, :], AF.Relu,
                                                 bias=bdtc[:, g:g + 1]), "sv", 1)
                        cx.wait(scalar, "vv", bb + 1)
                        if BL >= 1:
                            cx.wait(scalar, "vv", blockbase(BL - 1) + 34)
                        for n in range(NST):
                            if n >= 2:
                                cx.wait(scalar, "vv", bb + 4 + 2 * (n - 2))
                            cx.inc(scalar.activation(a_sb[n % 2][:, :], delta_sb[BL % 2][:, :],
                                                     AF.Exp, scale=Aneg[:, g * NST + n:g * NST + n + 1]),
                                   "sa", 1)

        @block.tensor
        def _(tensor):
            cx.barrier_at(tensor, snap)
            for c8 in range(NC8):
                cx.wait(tensor, "vv", vv0 + c8 * NV + NPRE)
                for c2 in range(4):
                    for n in range(NST):
                        E = (c8 * 4 + c2) * 16 + n
                        if E >= 2:
                            cx.wait(tensor, "svc", E - 1)
                        inst = tensor.matmul(p_cc[n % 2][:, :], oh_sb[:, (NST + n) * 128:(NST + n + 1) * 128],
                                             bc_sb[:, c2 * 512:(c2 + 1) * 512], start=True, stop=True)
                        cx.inc(inst, "tc", 1)
                    for g in range(2):
                        BL = c8 * 8 + c2 * 2 + g
                        bb = blockbase(BL)
                        if BL >= 2:
                            cx.wait(tensor, "sv", sv0 + 4 * (BL - 1))
                        inst = tensor.matmul(p_d[BL % 2][:, :], wdtb[:, g * 128:(g + 1) * 128],
                                             dtb[:, c2 * 512:(c2 + 1) * 512], start=True, stop=True)
                        cx.inc(inst, "td", 1)
                        for n in range(NST):
                            if n >= 2:
                                cx.wait(tensor, "vv", bb + 3 + 2 * (n - 2))
                            elif BL >= 1:
                                cx.wait(tensor, "vv", blockbase(BL - 1) + 3 + 2 * (n + 14))
                            inst = tensor.matmul(p_bc[n % 2][:, :], oh_sb[:, n * 128:(n + 1) * 128],
                                                 bc_sb[:, c2 * 512:(c2 + 1) * 512], start=True, stop=True)
                            cx.inc(inst, "tb", 1)

        block_cm.__exit__(None, None, None)
        _finalize(nc, cx)



def build_stageD(tl=TL, upto=7):
    nc = bass.Bass("TRN2", target_bir_lowering=False)
    io = {}
    io["y_i"] = nc.dram_tensor("y_i", [DI, tl], BF16, kind="ExternalInput").ap()
    io["xT"] = nc.dram_tensor("xT", [D, tl], F32, kind="ExternalInput").ap()
    io["xn_i"] = nc.dram_tensor("xn_i", [D, tl], BF16, kind="ExternalInput").ap()
    io["ffn0_i"] = nc.dram_tensor("ffn0_i", [D, tl], F32, kind="ExternalInput").ap()
    io["woutT"] = nc.dram_tensor("woutT", [DI, D], F32, kind="ExternalInput").ap()
    io["Wq"] = nc.dram_tensor("Wq", [D, D], F32, kind="ExternalInput").ap()
    io["Wk"] = nc.dram_tensor("Wk", [D, D], F32, kind="ExternalInput").ap()
    io["Wv"] = nc.dram_tensor("Wv", [D, D], F32, kind="ExternalInput").ap()
    io["bq"] = nc.dram_tensor("bq", [D, 1], F32, kind="ExternalInput").ap()
    io["bk"] = nc.dram_tensor("bk", [D, 1], F32, kind="ExternalInput").ap()
    io["bv"] = nc.dram_tensor("bv", [D, 1], F32, kind="ExternalInput").ap()
    io["routerT"] = nc.dram_tensor("routerT", [D, 32], F32, kind="ExternalInput").ap()
    io["rbias"] = nc.dram_tensor("rbias", [32, 1], F32, kind="ExternalInput").ap()
    io["gateT"] = nc.dram_tensor("gateT", [D, DH], F32, kind="ExternalInput").ap()
    io["upT"] = nc.dram_tensor("upT", [D, DH], F32, kind="ExternalInput").ap()
    io["downT"] = nc.dram_tensor("downT", [DH, D], F32, kind="ExternalInput").ap()
    io["sgT"] = nc.dram_tensor("sgT", [D, 512], F32, kind="ExternalInput").ap()
    io["suT"] = nc.dram_tensor("suT", [D, 512], F32, kind="ExternalInput").ap()
    io["sdT"] = nc.dram_tensor("sdT", [512, D], F32, kind="ExternalInput").ap()
    io["x1_o"] = nc.dram_tensor("x1_o", [D, tl], F32, kind="ExternalOutput").ap()
    io["xn1_o"] = nc.dram_tensor("xn1_o", [D, tl], BF16, kind="ExternalOutput").ap()
    io["q1_o"] = nc.dram_tensor("q1_o", [D, tl], BF16, kind="ExternalOutput").ap()
    io["kvp_o"] = nc.dram_tensor("kvp_o", [64, 1024], F32, kind="ExternalOutput").ap()
    io["ffn1_o"] = nc.dram_tensor("ffn1_o", [D, tl], F32, kind="ExternalOutput").ap()
    with ExitStack() as st:
        cx = Ctx(nc, st)
        cx.sem("ld", "wla", "wc", "tf", "sv", "vv", "so", "sa", "svc", "td", "tb", "tc", "cc")
        emit_stageD(nc, cx, io, tl, upto)
    return nc


def emit_stageD(nc, cx, io, tl=TL, upto=7):
    (y_i, xT, xn_i, ffn0_i, woutT, Wq, Wk, Wv, bq, bk, bv, routerT, rbias, gateT, upT, downT, sgT, suT, sdT, x1_o, xn1_o, q1_o, kvp_o, ffn1_o) = (io["y_i"], io["xT"], io["xn_i"], io["ffn0_i"], io["woutT"], io["Wq"], io["Wk"], io["Wv"], io["bq"], io["bk"], io["bv"], io["routerT"], io["rbias"], io["gateT"], io["upT"], io["downT"], io["sgT"], io["suT"], io["sdT"], io["x1_o"], io["xn1_o"], io["q1_o"], io["kvp_o"], io["ffn1_o"])
    bv_ = bv

    NC4 = tl // 512
    NTT = tl // 128

    hm = np.zeros((8, 128, 16), np.float32)
    hmT = np.zeros((8, 16, 128), np.float32)
    for m in range(8):
        for p in range(128):
            h = (m * 128 + p) // 64
            hm[m, p, h] = 1.0
            hmT[m, h, p] = 1.0
    ident = np.eye(128, dtype=np.float32)

    with cx.stage("D_"):
        st = cx.st
        consts = cx.sb("consts", [128, 4], F32)
        constd = nc.inline_tensor(np.tile(np.array([[1e-5, 1e-6, 1.0, 0.0]], np.float32), (128, 1)),
                                  name="constdD").ap()
        biases = cx.sb("biases", [128, 24], F32)
        rb_sb = cx.sb("rb_sb", [32, 1], F32)
        hm_sb = cx.sb("hm_sb", [128, 8 * 16], F32)
        hmT_sb = cx.sb("hmT_sb", [16, 8 * 128], F32)
        id_sb = cx.sb("id_sb", [128, 128], BF16)
        hmd = nc.inline_tensor(hm.transpose(1, 0, 2).reshape(128, 8 * 16), name="hmd").ap()
        hmTd = nc.inline_tensor(hmT.transpose(1, 0, 2).reshape(16, 8 * 128), name="hmTd").ap()
        identd = nc.inline_tensor(ident.astype(BF), name="identd").ap()

        with nc.Block() as b0:
            @b0.sync
            def _(sync):
                _n = nc.allow_non_contiguous_dma(reason="small vectors")
                _n.__enter__()
                cx.inc(sync.dma_start(out=consts[:, :], in_=constd), "ld", 16)
                cx.inc(sync.dma_start(out=biases[:, 0:8].rearrange("p (a o) -> p a o", o=1),
                                      in_=bq.rearrange("(a p) o -> p a o", p=128)), "ld", 16)
                cx.inc(sync.dma_start(out=biases[:, 8:16].rearrange("p (a o) -> p a o", o=1),
                                      in_=bk.rearrange("(a p) o -> p a o", p=128)), "ld", 16)
                cx.inc(sync.dma_start(out=biases[:, 16:24].rearrange("p (a o) -> p a o", o=1),
                                      in_=bv_.rearrange("(a p) o -> p a o", p=128)), "ld", 16)
                cx.inc(sync.dma_start(out=rb_sb[:, :], in_=rbias), "ld", 16)
                cx.inc(sync.dma_start(out=hm_sb[:, :], in_=hmd), "ld", 16)
                cx.inc(sync.dma_start(out=hmT_sb[:, :], in_=hmTd), "ld", 16)
                cx.inc(sync.dma_start(out=id_sb[:, :], in_=identd), "ld", 16)
                _n.__exit__(None, None, None)

        s12 = st.enter_context(ExitStack())
        x1f = cx.sb("x1f_main", [128, 8 * tl], F32, s12)
        with ExitStack() as ph:
            ysb = cx.sb("ysb", [128, 16 * tl], BF16, ph)
            wfD = [cx.sb(f"wfD{i}", [128, 2048], F32, ph) for i in range(2)]
            wbD = [cx.sb(f"wbD{i}", [128, 2048], BF16, ph) for i in range(2)]
            xres = [cx.sb(f"xres{i}", [128, tl], F32, ph) for i in range(2)]
            fres = [cx.sb(f"fres{i}", [128, tl], F32, ph) for i in range(2)]
            nres = [cx.sb(f"nres{i}", [128, tl], BF16, ph) for i in range(2)]
            ppD = [cx.ps(f"ppD{i}", [128, 512], ph) for i in range(8)]
            snap = cx.snapshot()
            wl0, wc0, tf0, sv0, vv0, so0 = (cx.n("wla"), cx.n("wc"), cx.n("tf"),
                                            cx.n("sv"), cx.n("vv"), cx.n("so"))
            with nc.Block() as blk:
                @blk.sync
                def _(sync):
                    cx.barrier_at(sync, snap)
                    if io.get("y_sliced_by_core", False):
                        for arm in nc.switch_core_id(engines=[sync], n=128):
                            cid = arm.logical % NCORES
                            sync.dma_start(
                                out=ysb[:, :].rearrange("p (a t) -> p a t", a=16),
                                in_=y_i.rearrange("(a p) t -> p a t", p=128)[:, :, cid * TL:(cid + 1) * TL]
                            ).then_inc(cx.sems["wla"][0], 16)
                        cx.sems["wla"][1] += 16
                    else:
                        cx.inc(sync.dma_start(out=ysb[:, :].rearrange("p (a t) -> p a t", a=16),
                                              in_=y_i.rearrange("(a p) t -> p a t", p=128)), "wla", 16)
                    for m in range(8):
                        if m >= 2:
                            cx.wait(sync, "wc", wc0 + m - 1)
                            cx.wait(sync, "vv", vv0 + (m - 2) * 3 + 3)
                        cx.inc(sync.dma_start(out=wfD[m % 2][:, :].rearrange("p (a q) -> p a q", q=128),
                                              in_=woutT[:, m * 128:(m + 1) * 128]
                                              .rearrange("(a p) q -> p a q", p=128)), "wla", 16)
                        cx.inc(sync.dma_start(out=xres[m % 2][:, :], in_=xT[m * 128:(m + 1) * 128, :]), "wla", 16)
                        cx.inc(sync.dma_start(out=fres[m % 2][:, :], in_=ffn0_i[m * 128:(m + 1) * 128, :]), "wla", 16)
                        cx.inc(sync.dma_start(out=nres[m % 2][:, :], in_=xn_i[m * 128:(m + 1) * 128, :]), "wla", 16)
                    cx.wait(sync, "vv", vv0 + 24)
                    cx.inc(sync.dma_start(out=x1_o.rearrange("(a p) t -> p a t", p=128),
                                          in_=x1f[:, :].rearrange("p (a t) -> p a t", a=8)), "so", 16)

                @blk.vector
                def _(vector):
                    cx.barrier_at(vector, snap)
                    for m in range(8):
                        cx.wait(vector, "wla", wl0 + 16 * (5 + 4 * m))
                        cx.wait(vector, "sv", sv0 + 4 * (m + 1))
                        sl = slice(m * tl, (m + 1) * tl)
                        cx.inc(vector.tensor_tensor(x1f[:, sl], x1f[:, sl], xres[m % 2][:, :],
                                                    ALU.add), "vv", 1)
                        cx.inc(vector.tensor_tensor(x1f[:, sl], x1f[:, sl], nres[m % 2][:, :],
                                                    ALU.add), "vv", 1)
                        cx.inc(vector.tensor_tensor(x1f[:, sl], x1f[:, sl], fres[m % 2][:, :],
                                                    ALU.add), "vv", 1)

                @blk.tensor
                def _(tensor):
                    cx.barrier_at(tensor, snap)
                    for m in range(8):
                        cx.wait(tensor, "wc", wc0 + m + 1)
                        if m >= 2:
                            cx.wait(tensor, "sv", sv0 + 4 * (m - 1))
                        inst = None
                        for k in range(16):
                            for c2 in range(4):
                                inst = tensor.matmul(ppD[(m % 2) * 4 + c2][:, :],
                                                     wbD[m % 2][:, k * 128:(k + 1) * 128],
                                                     ysb[:, k * tl + c2 * 512:k * tl + (c2 + 1) * 512],
                                                     start=(k == 0), stop=(k == 15))
                        cx.inc(inst, "tf", 1)

                @blk.scalar
                def _(scalar):
                    cx.barrier_at(scalar, snap)
                    for m in range(8):
                        cx.wait(scalar, "wla", wl0 + 16 * (2 + 4 * m))
                        cx.inc(scalar.copy(wbD[m % 2][:, :], wfD[m % 2][:, :]), "wc", 1)
                        cx.wait(scalar, "tf", tf0 + m + 1)
                        for c2 in range(4):
                            cx.inc(scalar.activation(x1f[:, m * tl + c2 * 512:m * tl + (c2 + 1) * 512],
                                                     ppD[(m % 2) * 4 + c2][:, :], AF.Copy), "sv", 1)

        if upto < 2:
            _finalize(nc, cx)
            return
        xn1b = cx.sb("xn1b", [128, 8 * tl], BF16, side="right")
        ones1 = cx.sb("ones1D", [128, 1], F32, side="right")
        onesr = cx.sb("onesrD", [1, 128], F32, side="right")
        with nc.Block() as bl1:
            @bl1.sync
            def _(sync):
                onesc = nc.inline_tensor(np.ones((1, 128), np.float32), name="onescD").ap()
                onescol = nc.inline_tensor(np.ones((128, 1), np.float32), name="onescolD").ap()
                cx.inc(sync.dma_start(out=onesr[:, :], in_=onesc), "ld", 16)
                cx.inc(sync.dma_start(out=ones1[:, :], in_=onescol), "ld", 16)
        import os as _os
        if _os.environ.get("D2MODE", "ln") == "ln":
            with ExitStack() as ph:
                emit_ln_phase(nc, cx, ph, tl, x1f, xn1b, None, ones1, onesr, consts[:, 0:1], rms=False)
        else:
            with nc.Block() as bcp:
                @bcp.scalar
                def _(scalar):
                    cx.barrier_at(scalar, cx.snapshot())
                    for k in range(8):
                        cx.inc(scalar.activation(xn1b[:, k * tl:(k + 1) * tl],
                                                 x1f[:, k * tl:(k + 1) * tl], AF.Copy), "sv", 1)
        with nc.Block() as bl2:
            @bl2.sync
            def _(sync):
                cx.wait(sync, "sv", cx.n("sv"))
                cx.inc(sync.dma_start(out=xn1_o.rearrange("(a p) t -> p a t", p=128),
                                      in_=xn1b[:, :].rearrange("p (a t) -> p a t", a=8)), "so", 16)

        s12.close()

        if upto < 3:
            _finalize(nc, cx)
            return
        _qkv_phase(nc, cx, st, tl, Wq, Wk, Wv, biases, xn1b, q1_o, kvp_o,
                   hm_sb, hmT_sb, id_sb, consts, ones1, onesr)

        if upto < 4:
            _finalize(nc, cx)
            return
        _moe_phase(nc, cx, st, tl, routerT, rb_sb, gateT, upT, downT, sgT, suT, sdT,
                   xn1b, ffn1_o, id_sb, consts, upto=upto)
        _finalize(nc, cx)



def _qkv_phase(nc, cx, st, tl, Wq, Wk, Wv, biases, xn1b, q1_o, kvp_o,
               hm_sb, hmT_sb, id_sb, consts, ones1, onesr):
    NC4 = tl // 512
    NTT = tl // 128
    jobs = [(w, m) for w in range(3) for m in range(8)]
    import os as _os
    D3MODE = _os.environ.get("D3MODE", "full")

    evt = {}
    cvv = 0
    cwc = 0
    vorder = []
    for j, (w, m) in enumerate(jobs):
        vorder.append(("cast", j))
        if j - 2 >= 0:
            vorder.append(("elu", j - 2))
    vorder += [("elu", 22), ("elu", 23)]
    if D3MODE != "qkv":
        for c2 in range(NC4):
            vorder.append(("rinv", c2))
            for m in range(8):
                vorder.append(("khat", c2, m))
    for item in vorder:
        if item[0] == "cast":
            cwc += 1
            evt[("cast", item[1])] = cwc
        elif item[0] == "elu":
            j = item[1]
            w = jobs[j][0]
            if w == 2:
                evt[("elu", j)] = cvv
                continue
            for c2 in range(4):
                cvv += 1
                evt[("mn", j, c2)] = cvv
                cvv += 1
                evt[("add", j, c2)] = cvv
            evt[("elu", j)] = cvv
        elif item[0] == "rinv":
            cvv += 1
            evt[("rinv", item[1])] = cvv
        elif item[0] == "khat":
            cvv += 1
            evt[("khat", item[1], item[2])] = cvv
    VTOT = cvv

    sevt = {}
    c = 0
    for j, (w, m) in enumerate(jobs):
        for c2 in range(4):
            c += 2
            sevt[("ev", j, c2)] = c
    if D3MODE != "qkv":
        for c2 in range(NC4):
            c += 8
            sevt[("ksq", c2)] = c
            c += 1
            sevt[("sqrt", c2)] = c
    if D3MODE == "full":
        for ts_ in range(NTT):
            for m in range(8):
                c += 2
                sevt[("trev", ts_, m)] = c
    STOT = c

    tevt = {}
    c = 0
    for j in range(24):
        c += 1
        tevt[("mm", j)] = c
    if D3MODE != "qkv":
        for c2 in range(NC4):
            c += 8
            tevt[("mask", c2)] = c
            for m in range(8):
                c += 1
                tevt[("kbc", c2, m)] = c
    if D3MODE == "full":
        for ts_ in range(NTT):
            for m in range(8):
                c += 2
                tevt[("tr", ts_, m)] = c
            c += 16
            tevt[("kvmm", ts_)] = c
    TTOT = c

    with ExitStack() as ph:
        k1b = cx.sb("k1b", [128, 8 * tl], BF16, ph)
        vb = cx.sb("vb", [128, 8 * tl], BF16, ph)
        wfQ = [cx.sb(f"wfQ{i}", [128, 1024], F32, ph) for i in range(2)]
        wbQ = [cx.sb(f"wbQ{i}", [128, 1024], BF16, ph) for i in range(2)]
        rt = [cx.sb(f"rtq{i}", [128, 512], F32, ph) for i in range(2)]
        et = [cx.sb(f"etq{i}", [128, 512], F32, ph) for i in range(2)]
        mnt = [cx.sb(f"mnt{i}", [128, 512], F32, ph) for i in range(2)]
        q1t = [cx.sb(f"q1t{i}", [128, tl], BF16, ph) for i in range(2)]
        ksq = cx.sb("ksq", [128, 8 * 512], F32, ph)
        rinv = cx.sb("rinv", [16, 512], F32, ph)
        kT_sb = cx.sb("kT_sb", [128, 1024], BF16, ph)
        vT_sb = cx.sb("vT_sb", [128, 1024], BF16, ph)
        kvf = cx.sb("kvf", [64, 1024], F32, ph)

        snap = cx.snapshot()
        wl0, wc0, tf0, sv0, vv0, so0 = (cx.n("wla"), cx.n("wc"), cx.n("tf"),
                                        cx.n("sv"), cx.n("vv"), cx.n("so"))
        WAP = [Wq, Wk, Wv]
        ph1 = ph.enter_context(ExitStack()) if False else ExitStack()
        ph1.__enter__()
        ppQ = [cx.ps(f"ppQ{i}", [128, 512], ph1) for i in range(4)]
        p_hs = cx.ps("p_hs", [16, 512], ph1)
        p_kb = cx.ps("p_kb", [128, 512], ph1)

        with nc.Block() as blk:
            @blk.sync
            def _(sync):
                cx.barrier_at(sync, snap)
                for j, (w, m) in enumerate(jobs):
                    if j >= 2:
                        cx.wait(sync, "wc", wc0 + j - 1)
                    cx.inc(sync.dma_start(out=wfQ[j % 2][:, :].rearrange("p (a q) -> p a q", q=128),
                                          in_=WAP[w][:, m * 128:(m + 1) * 128]
                                          .rearrange("(a p) q -> p a q", p=128)), "wla", 16)
                    if w == 0 and m >= 2:
                        cx.wait(sync, "vv", vv0 + evt[("add", m - 2, 3)])
                        cx.inc(sync.dma_start(out=q1_o[(m - 2) * 128:(m - 1) * 128, :],
                                              in_=q1t[m % 2][:, :]), "so", 16)
                for m in (6, 7):
                    cx.wait(sync, "vv", vv0 + evt[("add", m, 3)])
                    cx.inc(sync.dma_start(out=q1_o[m * 128:(m + 1) * 128, :],
                                          in_=q1t[m % 2][:, :]), "so", 16)

            @blk.vector
            def _(vector):
                cx.barrier_at(vector, snap)
                for item in vorder:
                    if item[0] == "cast":
                        j = item[1]
                        cx.wait(vector, "wla", wl0 + 16 * (j + 1))
                        if j >= 2:
                            cx.wait(vector, "tf", tf0 + tevt[("mm", j - 2)])
                        cx.inc(vector.tensor_copy(wbQ[j % 2][:, :], wfQ[j % 2][:, :]), "wc", 1)
                    elif item[0] == "elu":
                        j = item[1]
                        w, m = jobs[j]
                        if w == 2:
                            continue
                        for c2 in range(4):
                            bcol = biases[:, w * 8 + m:w * 8 + m + 1]
                            if w == 0 and c2 == 0 and m >= 2:
                                cx.wait(vector, "so", so0 + 16 * (m - 1))
                            cx.wait(vector, "sv", sv0 + sevt[("ev", j, c2)] - 1)
                            cx.inc(vector.tensor_scalar(mnt[c2 % 2][:, :], ppQ[c2][:, :], bcol, 0.0,
                                                        ALU.add, ALU.min), "vv", 1)
                            cx.wait(vector, "sv", sv0 + sevt[("ev", j, c2)])
                            if w == 0:
                                cx.inc(vector.tensor_tensor(q1t[m % 2][:, c2 * 512:(c2 + 1) * 512],
                                                            et[c2 % 2][:, :], rt[c2 % 2][:, :], ALU.add), "vv", 1)
                            else:
                                cx.inc(vector.tensor_tensor(
                                    k1b[:, m * tl + c2 * 512:m * tl + (c2 + 1) * 512],
                                    et[c2 % 2][:, :], rt[c2 % 2][:, :], ALU.add), "vv", 1)
                    elif item[0] == "rinv":
                        c2 = item[1]
                        cx.wait(vector, "sv", sv0 + sevt[("sqrt", c2)])
                        if c2 >= 1:
                            cx.wait(vector, "tf", tf0 + tevt[("kbc", c2 - 1, 7)])
                        cx.inc(vector.reciprocal(rinv[:, :], ksq[0:16, 0:512]), "vv", 1)
                    else:
                        _, c2, m = item
                        cx.wait(vector, "tf", tf0 + tevt[("kbc", c2, m)])
                        sl = slice(m * tl + c2 * 512, m * tl + (c2 + 1) * 512)
                        cx.inc(vector.tensor_tensor(k1b[:, sl], k1b[:, sl], p_kb[:, :], ALU.mult), "vv", 1)

            @blk.scalar
            def _(scalar):
                cx.barrier_at(scalar, snap)
                for j, (w, m) in enumerate(jobs):
                    cx.wait(scalar, "tf", tf0 + tevt[("mm", j)])
                    bcol = biases[:, w * 8 + m:w * 8 + m + 1]
                    for c2 in range(4):
                        if w == 2:
                            cx.inc(scalar.activation(vb[:, m * tl + c2 * 512:m * tl + (c2 + 1) * 512],
                                                     ppQ[c2][:, :], AF.Identity, bias=bcol), "sv", 2)
                        else:
                            cx.inc(scalar.activation(rt[c2 % 2][:, :], ppQ[c2][:, :], AF.Relu,
                                                     bias=bcol), "sv", 1)
                            cx.wait(scalar, "vv", vv0 + evt[("mn", j, c2)])
                            cx.inc(scalar.activation(et[c2 % 2][:, :], mnt[c2 % 2][:, :], AF.Exp), "sv", 1)
                for c2 in (range(NC4) if D3MODE != "qkv" else []):
                    cx.wait(scalar, "vv", vv0 + evt[("add", 15, 3)])
                    if c2 >= 1:
                        cx.wait(scalar, "tf", tf0 + tevt[("mask", c2 - 1)])
                        cx.wait(scalar, "vv", vv0 + evt[("rinv", c2 - 1)])
                    for m in range(8):
                        cx.inc(scalar.activation(ksq[:, m * 512:(m + 1) * 512],
                                                 k1b[:, m * tl + c2 * 512:m * tl + (c2 + 1) * 512],
                                                 AF.Square), "sv", 1)
                    cx.wait(scalar, "tf", tf0 + tevt[("mask", c2)])
                    cx.inc(scalar.activation(ksq[0:16, 0:512], p_hs[:, :], AF.Sqrt,
                                             bias=consts[0:16, 1:2]), "sv", 1)

            @blk.tensor
            def _(tensor):
                cx.barrier_at(tensor, snap)
                for j, (w, m) in enumerate(jobs):
                    cx.wait(tensor, "wc", wc0 + evt[("cast", j)])
                    if j >= 1:
                        cx.wait(tensor, "sv", sv0 + sevt[("ev", j - 1, 3)])
                        if jobs[j - 1][0] != 2:
                            cx.wait(tensor, "vv", vv0 + evt[("mn", j - 1, 3)])
                    inst = None
                    for k in range(8):
                        for c2 in range(4):
                            inst = tensor.matmul(ppQ[c2][:, :], wbQ[j % 2][:, k * 128:(k + 1) * 128],
                                                 xn1b[:, k * tl + c2 * 512:k * tl + (c2 + 1) * 512],
                                                 start=(k == 0), stop=(k == 7))
                    cx.inc(inst, "tf", 1)
                for c2 in (range(NC4) if D3MODE != "qkv" else []):
                    cx.wait(tensor, "sv", sv0 + sevt[("ksq", c2)])
                    inst = None
                    for m in range(8):
                        inst = tensor.matmul(p_hs[:, :], hm_sb[:, m * 16:(m + 1) * 16],
                                             ksq[:, m * 512:(m + 1) * 512],
                                             start=(m == 0), stop=(m == 7))
                    cx.inc(inst, "tf", 8)
                    cx.wait(tensor, "vv", vv0 + evt[("rinv", c2)])
                    for m in range(8):
                        if (c2, m) != (0, 0):
                            pm = (c2, m - 1) if m > 0 else (c2 - 1, 7)
                            cx.wait(tensor, "vv", vv0 + evt[("khat", pm[0], pm[1])])
                        inst = tensor.matmul(p_kb[:, :], hmT_sb[:, m * 128:(m + 1) * 128],
                                             rinv[:, :], start=True, stop=True)
                        cx.inc(inst, "tf", 1)

        ph1.__exit__(None, None, None)
        if D3MODE != "full":
            return
        with ExitStack() as ph2:
            p_tr = [cx.ps(f"p_tr{i}", [128, 128], ph2, dtype=BF16) for i in range(2)]
            p_kv = [cx.ps(f"p_kv{i}", [64, 512], ph2) for i in range(2)]
            snap2 = cx.snapshot()
            with nc.Block() as blk2:
                @blk2.sync
                def _(sync):
                    cx.barrier_at(sync, snap2)
                    cx.wait(sync, "vv", vv0 + VTOT + 1 + 2 * NTT)
                    cx.inc(sync.dma_start(out=kvp_o, in_=kvf[:, :]), "so", 16)

                @blk2.scalar
                def _(scalar):
                    cx.barrier_at(scalar, snap2)
                    for ts_ in range(NTT):
                        for m in range(8):
                            cx.wait(scalar, "tf", tf0 + tevt[("tr", ts_, m)])
                            cx.inc(scalar.activation(kT_sb[:, m * 128:(m + 1) * 128],
                                                     p_tr[0][:, :], AF.Copy), "sv", 1)
                            cx.inc(scalar.activation(vT_sb[:, m * 128:(m + 1) * 128],
                                                     p_tr[1][:, :], AF.Copy), "sv", 1)

                @blk2.vector
                def _(vector):
                    cx.barrier_at(vector, snap2)
                    cx.inc(vector.memset(kvf[:, :], 0.0), "vv", 1)
                    for ts_ in range(NTT):
                        cx.wait(vector, "tf", tf0 + tevt[("kvmm", ts_)])
                        cx.inc(vector.tensor_tensor(kvf[:, 0:512], kvf[:, 0:512],
                                                    p_kv[0][0:64, :], ALU.add), "vv", 1)
                        cx.inc(vector.tensor_tensor(kvf[:, 512:1024], kvf[:, 512:1024],
                                                    p_kv[1][0:64, :], ALU.add), "vv", 1)

                @blk2.tensor
                def _(tensor):
                    cx.barrier_at(tensor, snap2)
                    for ts_ in range(NTT):
                        for m in range(8):
                            if (ts_, m) != (0, 0):
                                pi = ts_ * 8 + m - 1
                                cx.wait(tensor, "sv", sv0 + sevt[("trev", pi // 8, pi % 8)])
                            i1 = tensor.transpose(p_tr[0][:, :],
                                                  k1b[:, m * tl + ts_ * 128:m * tl + (ts_ + 1) * 128],
                                                  id_sb[:, :])
                            cx.inc(i1, "tf", 1)
                            i2 = tensor.transpose(p_tr[1][:, :],
                                                  vb[:, m * tl + ts_ * 128:m * tl + (ts_ + 1) * 128],
                                                  id_sb[:, :])
                            cx.inc(i2, "tf", 1)
                        cx.wait(tensor, "sv", sv0 + sevt[("trev", ts_, 7)])
                        cx.wait(tensor, "vv", vv0 + VTOT + 1 + 2 * ts_)
                        for h in range(16):
                            inst = tensor.matmul(p_kv[h // 8][0:64, (h % 8) * 64:(h % 8) * 64 + 64],
                                                 kT_sb[:, h * 64:(h + 1) * 64], vT_sb[:, h * 64:(h + 1) * 64],
                                                 start=True, stop=True,
                                                 skip_group_check=True)
                            cx.inc(inst, "tf", 1)



def _moe_phase(nc, cx, st, tl, routerT, rb_sb, gateT, upT, downT, sgT, suT, sdT,
               xn1b, ffn1_o, id_sb, consts, upto=7):
    NC4 = tl // 512
    NTT = tl // 128
    oh32 = np.zeros((32, 32 * 128), np.float32)
    for e in range(32):
        oh32[e, e * 128:(e + 1) * 128] = 1.0
    hs_dram = nc.dram_tensor("hs_scratch", [DH, tl], BF16).ap()

    with ExitStack() as ph:
        idf = cx.sb("idf", [128, 128], F32, ph)
        oh_sb = cx.sb("oh32_sb", [32, 32 * 128], F32, ph)
        w4T = cx.sb("w4T", [32, tl], F32, ph)
        rtb = cx.sb("rtb", [128, 8 * 32], BF16, ph)
        rtf = cx.sb("rtf", [128, 8 * 32], F32, ph)
        shb = cx.sb("shb", [128, 4 * tl], BF16, ph)
        wfM = [cx.sb(f"wfM{i}", [128, 1024], F32, ph) for i in range(2)]
        wbM = [cx.sb(f"wbM{i}", [128, 1024], BF16, ph) for i in range(2)]
        t1m = cx.sb("t1m", [128, tl], BF16, ph)
        t2m = cx.sb("t2m", [128, tl], BF16, ph)
        h1m = cx.sb("h1m", [128, tl], BF16, ph)
        hsm = [cx.sb(f"hsm{i}", [128, tl], BF16, ph) for i in range(2)]
        ph0 = ExitStack()
        ph0.__enter__()
        lg = cx.sb("lg", [32, tl], F32, ph0)
        lgT = cx.sb("lgT", [128, NTT * 32], F32, ph0)
        z4 = cx.sb("z4", [128, 512], F32, ph0)
        w4 = cx.sb("w4", [128, 512], F32, ph0)
        ge4 = cx.sb("ge4", [128, 512], F32, ph0)
        tt4 = cx.sb("tt4", [128, 512], F32, ph0)
        st4m = cx.sb("st4m", [128, 64], F32, ph0)
        st4n = cx.sb("st4n", [128, 64], F32, ph0)
        st4z = cx.sb("st4z", [128, 64], F32, ph0)
        st4s = cx.sb("st4s", [128, 64], F32, ph0)
        st4r = cx.sb("st4r", [128, 64], F32, ph0)
        ppM = [cx.ps(f"ppM{i}", [128, 512], ph) for i in range(7)]
        p_lt = cx.ps("p_lt", [128, 512], ph)
        ppM.append(p_lt)

        idfd = nc.inline_tensor(np.eye(128, dtype=np.float32), name="idfd").ap()
        ohd32 = nc.inline_tensor(oh32, name="ohd32").ap()

        snap = cx.snapshot()
        z4w = z4
        with nc.Block() as b:
            @b.sync
            def _(sync):
                cx.barrier_at(sync, snap)
                cx.inc(sync.dma_start(out=idf[:, :], in_=idfd), "wla", 16)
                cx.inc(sync.dma_start(out=oh_sb[:, :], in_=ohd32), "wla", 16)
                cx.inc(sync.dma_start(out=rtf[:, :].rearrange("p (a q) -> p a q", q=32),
                                      in_=routerT.rearrange("(a p) q -> p a q", p=128)), "wla", 16)

            @b.vector
            def _(vector):
                cx.barrier_at(vector, snap)
                cx.wait(vector, "wla", snap["wla"] + 48)
                cx.inc(vector.tensor_copy(rtb[:, :], rtf[:, :]), "wc", 1)
                for c2 in range(NC4):
                    cx.wait(vector, "tf", snap["tf"] + c2 + 1)
                    cx.inc(vector.tensor_scalar(lg[:, c2 * 512:(c2 + 1) * 512], ppM[c2 % 2][0:32, :],
                                                rb_sb[:, 0:1], None, ALU.add), "vv", 1)
                for ts_ in range(NTT):
                    cx.wait(vector, "tf", snap["tf"] + NC4 + ts_ + 1)
                    cx.inc(vector.tensor_copy(lgT[:, ts_ * 32:(ts_ + 1) * 32], p_lt[:, 0:32]), "vv", 1)
                for ts_ in range(NTT):
                    vector.tensor_reduce(st4m[:, ts_ * 4:ts_ * 4 + 1], lgT[:, ts_ * 32:(ts_ + 1) * 32],
                                         mybir.AxisListType.X, ALU.max)
                for ts_ in range(NTT):
                    inst = vector.tensor_scalar(st4n[:, ts_ * 4:ts_ * 4 + 1], st4m[:, ts_ * 4:ts_ * 4 + 1], -1.0, None, ALU.mult)
                cx.inc(inst, "vv", 1)
                cx.wait(vector, "sv", snap["sv"] + 16)
                vector.memset(w4[:, :], 0.0)
                for it in range(4):
                    for ts_ in range(NTT):
                        vector.tensor_reduce(st4z[:, ts_ * 4:ts_ * 4 + 1], z4[:, ts_ * 32:(ts_ + 1) * 32],
                                             mybir.AxisListType.X, ALU.max)
                    for ts_ in range(NTT):
                        vector.tensor_scalar(ge4[:, ts_ * 32:(ts_ + 1) * 32], z4[:, ts_ * 32:(ts_ + 1) * 32],
                                             st4z[:, ts_ * 4:ts_ * 4 + 1], None, ALU.is_ge)
                    for ts_ in range(NTT):
                        vector.tensor_tensor(tt4[:, ts_ * 32:(ts_ + 1) * 32], ge4[:, ts_ * 32:(ts_ + 1) * 32],
                                             z4[:, ts_ * 32:(ts_ + 1) * 32], ALU.mult)
                    vector.tensor_tensor(w4[:, :], w4[:, :], tt4[:, :], ALU.add)
                    vector.tensor_scalar(ge4[:, :], ge4[:, :], -1.0, 1.0, ALU.mult, ALU.add)
                    vector.tensor_tensor(z4[:, :], z4[:, :], ge4[:, :], ALU.mult)
                for ts_ in range(NTT):
                    inst = vector.tensor_reduce(st4s[:, ts_ * 4:ts_ * 4 + 1], w4[:, ts_ * 32:(ts_ + 1) * 32],
                                                mybir.AxisListType.X, ALU.add)
                cx.inc(inst, "vv", 1)
                cx.wait(vector, "vv", cx.n("vv"))
                cx.inc(vector.reciprocal(st4r[:, :].rearrange("p (a b) -> p a b", b=4)[:, :, 0],
                         st4s[:, :].rearrange("p (a b) -> p a b", b=4)[:, :, 0]), "vv", 1)
                cx.wait(vector, "vv", cx.n("vv"))
                for ts_ in range(NTT):
                    inst = vector.tensor_scalar(lgT[:, ts_ * 32:(ts_ + 1) * 32], w4[:, ts_ * 32:(ts_ + 1) * 32],
                                                st4r[:, ts_ * 4:ts_ * 4 + 1], None, ALU.mult)
                cx.inc(inst, "vv", 1)
                for ts_ in range(NTT):
                    cx.wait(vector, "tf", snap["tf"] + NC4 + NTT + ts_ + 1)
                    cx.inc(vector.tensor_copy(w4T[:, ts_ * 128:(ts_ + 1) * 128], p_lt[0:32, 0:128]), "vv", 1)

            @b.scalar
            def _(scalar):
                cx.barrier_at(scalar, snap)
                cx.wait(scalar, "vv", snap["vv"] + 21)
                for ts_ in range(NTT):
                    inst = scalar.activation(z4[:, ts_ * 32:(ts_ + 1) * 32], lgT[:, ts_ * 32:(ts_ + 1) * 32],
                                             AF.Exp, bias=st4n[:, ts_ * 4:ts_ * 4 + 1])
                cx.inc(inst, "sv", 16)

            @b.tensor
            def _(tensor):
                cx.barrier_at(tensor, snap)
                cx.wait(tensor, "wc", snap["wc"] + 1)
                for c2 in range(NC4):
                    if c2 >= 2:
                        cx.wait(tensor, "vv", snap["vv"] + c2 - 1)
                    inst = None
                    for k in range(8):
                        inst = tensor.matmul(ppM[c2 % 2][0:32, :], rtb[:, k * 32:(k + 1) * 32],
                                             xn1b[:, k * tl + c2 * 512:k * tl + (c2 + 1) * 512],
                                             start=(k == 0), stop=(k == 7))
                    cx.inc(inst, "tf", 1)
                for ts_ in range(NTT):
                    cx.wait(tensor, "vv", snap["vv"] + ts_ // 4 + 1)
                    if ts_ >= 1:
                        cx.wait(tensor, "vv", snap["vv"] + 4 + ts_)
                    inst = tensor.transpose(p_lt[:, 0:32], lg[:, ts_ * 128:(ts_ + 1) * 128], idf[0:32, 0:32])
                    cx.inc(inst, "tf", 1)
                for ts_ in range(NTT):
                    cx.wait(tensor, "vv", snap["vv"] + 24)
                    if ts_ >= 1:
                        cx.wait(tensor, "vv", snap["vv"] + 24 + ts_)
                    inst = tensor.transpose(p_lt[0:32, 0:128], lgT[:, ts_ * 32:(ts_ + 1) * 32], idf[:, :])
                    cx.inc(inst, "tf", 1)

        ph0.__exit__(None, None, None)
        if upto < 5:
            return
        snap1 = cx.snapshot()
        sj = []
        for m in range(4):
            sj.append(("sg", m, sgT))
            sj.append(("su", m, suT))
        with nc.Block() as b:
            @b.sync
            def _(sync):
                cx.barrier_at(sync, snap1)
                for j, (tag, m, ap) in enumerate(sj):
                    if j >= 2:
                        cx.wait(sync, "wc", snap1["wc"] + j - 1)
                    cx.inc(sync.dma_start(out=wfM[j % 2][:, :].rearrange("p (a q) -> p a q", q=128),
                                          in_=ap[:, m * 128:(m + 1) * 128]
                                          .rearrange("(a p) q -> p a q", p=128)), "wla", 16)

            @b.vector
            def _(vector):
                cx.barrier_at(vector, snap1)
                for j, (tag, m, ap) in enumerate(sj):
                    cx.wait(vector, "wla", snap1["wla"] + 16 * (j + 1))
                    if j >= 2:
                        cx.wait(vector, "tf", snap1["tf"] + j - 1)
                    cx.inc(vector.tensor_copy(wbM[j % 2][:, :], wfM[j % 2][:, :]), "wc", 1)
                    if tag == "su" and m >= 1:
                        cx.wait(vector, "sv", snap1["sv"] + m * 8)
                        cx.inc(vector.tensor_tensor(shb[:, (m - 1) * tl:m * tl], t1m[:, :], t2m[:, :],
                                                    ALU.mult), "vv", 1)
                cx.wait(vector, "sv", snap1["sv"] + 32)
                cx.inc(vector.tensor_tensor(shb[:, 3 * tl:4 * tl], t1m[:, :], t2m[:, :], ALU.mult), "vv", 1)

            @b.scalar
            def _(scalar):
                cx.barrier_at(scalar, snap1)
                for j, (tag, m, ap) in enumerate(sj):
                    cx.wait(scalar, "tf", snap1["tf"] + j + 1)
                    if m >= 1 and tag == "sg":
                        cx.wait(scalar, "vv", snap1["vv"] + m)
                    for c2 in range(4):
                        dst = t1m if tag == "sg" else t2m
                        fn = AF.Silu if tag == "sg" else AF.Copy
                        cx.inc(scalar.activation(dst[:, c2 * 512:(c2 + 1) * 512],
                                                 ppM[(j % 2) * 4 + c2][:, :], fn), "sv", 1)

            @b.tensor
            def _(tensor):
                cx.barrier_at(tensor, snap1)
                for j, (tag, m, ap) in enumerate(sj):
                    cx.wait(tensor, "wc", snap1["wc"] + j + 1)
                    if j >= 2:
                        cx.wait(tensor, "sv", snap1["sv"] + (j - 1) * 4)
                    inst = None
                    for k in range(8):
                        for c2 in range(4):
                            inst = tensor.matmul(ppM[(j % 2) * 4 + c2][:, :],
                                                 wbM[j % 2][:, k * 128:(k + 1) * 128],
                                                 xn1b[:, k * tl + c2 * 512:k * tl + (c2 + 1) * 512],
                                                 start=(k == 0), stop=(k == 7))
                    cx.inc(inst, "tf", 1)

        if upto < 6:
            return
        snap2 = cx.snapshot()
        gj = []
        for m in range(32):
            gj.append(("g", m, gateT))
            gj.append(("u", m, upT))

        def emit_hs(vector, mp):
            cx.wait(vector, "sv", snap2["sv"] + mp * 8 + 8)
            cx.inc(vector.tensor_tensor(h1m[:, :], t1m[:, :], t2m[:, :], ALU.mult), "vv", 1)
            for c2 in range(4):
                cx.wait(vector, "tf", snap2["tf"] + _moe_tf(2 * mp) + c2 + 2)
                if mp >= 2:
                    cx.wait(vector, "so", snap2["so"] + 16 * (mp - 1))
                cx.inc(vector.tensor_tensor(hsm[mp % 2][:, c2 * 512:(c2 + 1) * 512],
                                            h1m[:, c2 * 512:(c2 + 1) * 512],
                                            ppM[c2][:, :], ALU.mult), "vv", 1)

        with nc.Block() as b:
            @b.sync
            def _(sync):
                cx.barrier_at(sync, snap2)
                for j, (tag, m, ap) in enumerate(gj):
                    if j >= 2:
                        cx.wait(sync, "wc", snap2["wc"] + j - 1)
                    cx.inc(sync.dma_start(out=wfM[j % 2][:, :].rearrange("p (a q) -> p a q", q=128),
                                          in_=ap[:, m * 128:(m + 1) * 128]
                                          .rearrange("(a p) q -> p a q", p=128)), "wla", 16)
                    if tag == "u" and m >= 1:
                        mp = m - 1
                        cx.wait(sync, "vv", snap2["vv"] + mp * 5 + 5)
                        cx.inc(sync.dma_start(out=hs_dram[mp * 128:(mp + 1) * 128, :],
                                              in_=hsm[mp % 2][:, :]), "so", 16)
                cx.wait(sync, "vv", snap2["vv"] + 32 * 5)
                cx.inc(sync.dma_start(out=hs_dram[31 * 128:32 * 128, :], in_=hsm[31 % 2][:, :]), "so", 16)

            @b.vector
            def _(vector):
                cx.barrier_at(vector, snap2)
                for j, (tag, m, ap) in enumerate(gj):
                    cx.wait(vector, "wla", snap2["wla"] + 16 * (j + 1))
                    if j >= 2:
                        cx.wait(vector, "tf", snap2["tf"] + _moe_tf(j - 2))
                    cx.inc(vector.tensor_copy(wbM[j % 2][:, :], wfM[j % 2][:, :]), "wc", 1)
                    if tag == "u" and m >= 1:
                        emit_hs(vector, m - 1)
                emit_hs(vector, 31)

            @b.scalar
            def _(scalar):
                cx.barrier_at(scalar, snap2)
                for m in range(32):
                    cx.wait(scalar, "tf", snap2["tf"] + _moe_tf(2 * m))
                    if m >= 1:
                        cx.wait(scalar, "vv", snap2["vv"] + (m - 1) * 5 + 1)
                    for c2 in range(4):
                        cx.inc(scalar.activation(t1m[:, c2 * 512:(c2 + 1) * 512],
                                                 ppM[c2][:, :], AF.Silu), "sv", 1)
                    cx.wait(scalar, "tf", snap2["tf"] + _moe_tf(2 * m + 1) - 4)
                    for c2 in range(4):
                        cx.inc(scalar.activation(t2m[:, c2 * 512:(c2 + 1) * 512],
                                                 ppM[4 + c2][:, :], AF.Copy), "sv", 1)

            @b.tensor
            def _(tensor):
                cx.barrier_at(tensor, snap2)
                for j, (tag, m, ap) in enumerate(gj):
                    cx.wait(tensor, "wc", snap2["wc"] + j + 1)
                    if tag == "g":
                        if m >= 1:
                            cx.wait(tensor, "vv", snap2["vv"] + (m - 1) * 5 + 5)
                    else:
                        if m >= 1:
                            cx.wait(tensor, "sv", snap2["sv"] + (m - 1) * 8 + 8)
                    banks = [0, 1, 2, 3] if tag == "g" else [4, 5, 6, 7]
                    inst = None
                    for k in range(8):
                        for ci in range(4):
                            inst = tensor.matmul(ppM[banks[ci]][:, :],
                                                 wbM[j % 2][:, k * 128:(k + 1) * 128],
                                                 xn1b[:, k * tl + ci * 512:k * tl + (ci + 1) * 512],
                                                 start=(k == 0), stop=(k == 7))
                    cx.inc(inst, "tf", 1)
                    if tag == "u":
                        cx.wait(tensor, "sv", snap2["sv"] + m * 8 + 4)
                        for c2 in range(4):
                            inst = tensor.matmul(ppM[c2][:, :], oh_sb[:, m * 128:(m + 1) * 128],
                                                 w4T[:, c2 * 512:(c2 + 1) * 512], start=True, stop=True)
                            cx.inc(inst, "tf", 1)

        if upto < 7:
            return
        snap3 = cx.snapshot()
        with ExitStack() as ph3:
            dwf = cx.sb("dwf", [128, 4096], F32, ph3)
            dwb = [cx.sb(f"dwb{i}", [128, 4096], BF16, ph3) for i in range(2)]
            sdf = cx.sb("sdf", [128, 4096], F32, ph3)
            sdb = cx.sb("sdb", [128, 4096], BF16, ph3)
            hsk = [cx.sb(f"hsk{i}", [128, tl], BF16, ph3) for i in range(2)]
            fb = [cx.sb(f"fb{i}", [128, tl], F32, ph3) for i in range(2)]

            with nc.Block() as b:
                @b.sync
                def _(sync):
                    cx.barrier_at(sync, snap3)
                    cx.inc(sync.dma_start(out=sdf[:, :].rearrange("p (a q) -> p a q", q=1024),
                                          in_=sdT.rearrange("(a p) q -> p a q", p=128)), "wla", 16)
                    for mq in range(4):
                        for m2 in range(2):
                            if mq + m2 > 0:
                                cx.wait(sync, "wc", snap3["wc"] + 1 + mq * 2 + m2)
                            cx.inc(sync.dma_start(out=dwf[:, :].rearrange("p (a q) -> p a q", q=128),
                                                  in_=downT[:, (mq * 2 + m2) * 128:(mq * 2 + m2 + 1) * 128]
                                                  .rearrange("(a p) q -> p a q", p=128)), "wla", 16)
                        for k in range(32):
                            if mq * 32 + k >= 2:
                                cx.wait(sync, "tf", snap3["tf"] + 72 * mq + max(0, (k - 1)) * 2)
                            cx.inc(sync.dma_start(out=hsk[k % 2][:, :],
                                                  in_=hs_dram[k * 128:(k + 1) * 128, :]), "wla", 16)
                        cx.wait(sync, "sv", snap3["sv"] + (mq + 1) * 8)
                        for m2 in range(2):
                            cx.inc(sync.dma_start(out=ffn1_o[(mq * 2 + m2) * 128:(mq * 2 + m2 + 1) * 128, :],
                                                  in_=fb[m2][:, :]), "so", 16)

                @b.vector
                def _(vector):
                    cx.barrier_at(vector, snap3)
                    cx.wait(vector, "wla", snap3["wla"] + 16)
                    cx.inc(vector.tensor_copy(sdb[:, :], sdf[:, :]), "wc", 1)
                    for mq in range(4):
                        for m2 in range(2):
                            cx.wait(vector, "wla", snap3["wla"] + 16 * (1 + mq * 34 + m2 + 1))
                            if mq > 0:
                                cx.wait(vector, "tf", snap3["tf"] + 72 * mq)
                            cx.inc(vector.tensor_copy(dwb[m2][:, :], dwf[:, :]), "wc", 1)

                @b.scalar
                def _(scalar):
                    cx.barrier_at(scalar, snap3)
                    for mq in range(4):
                        cx.wait(scalar, "tf", snap3["tf"] + mq * 64 + 64 + (mq + 1) * 8)
                        if mq >= 2:
                            cx.wait(scalar, "so", snap3["so"] + 16 * 2 * (mq - 1))
                        for m2 in range(2):
                            for c2 in range(4):
                                cx.inc(scalar.activation(fb[m2][:, c2 * 512:(c2 + 1) * 512],
                                                         ppM[m2 * 4 + c2][:, :], AF.Copy), "sv", 1)

                @b.tensor
                def _(tensor):
                    cx.barrier_at(tensor, snap3)
                    for mq in range(4):
                        cx.wait(tensor, "wc", snap3["wc"] + 1 + mq * 2 + 2)
                        if mq >= 1:
                            cx.wait(tensor, "sv", snap3["sv"] + mq * 8)
                        for k in range(32):
                            cx.wait(tensor, "wla", snap3["wla"] + 16 * (1 + mq * 34 + 2 + k + 1))
                            inst = None
                            for m2 in range(2):
                                for c2 in range(4):
                                    inst = tensor.matmul(ppM[m2 * 4 + c2][:, :],
                                                         dwb[m2][:, k * 128:(k + 1) * 128],
                                                         hsk[k % 2][:, c2 * 512:(c2 + 1) * 512],
                                                         start=(k == 0), stop=False,
                                                         skip_group_check=True)
                            cx.inc(inst, "tf", 2)
                        for k4 in range(4):
                            for m2 in range(2):
                                for c2 in range(4):
                                    inst = tensor.matmul(ppM[m2 * 4 + c2][:, :],
                                                         sdb[:, k4 * 1024 + (mq * 2 + m2) * 128:
                                                             k4 * 1024 + (mq * 2 + m2 + 1) * 128],
                                                         shb[:, k4 * tl + c2 * 512:k4 * tl + (c2 + 1) * 512],
                                                         start=False, stop=(k4 == 3),
                                                         skip_group_check=True)
                            cx.inc(inst, "tf", 2)


def _moe_tf(j):
    n = 0
    for jj in range(j + 1):
        n += 1
        if jj % 2 == 1:
            n += 4
    return n



def build_stageE(tl=TL, nsrc=8):
    nc = bass.Bass("TRN2", target_bir_lowering=False)
    io = {}
    io["kvp"] = nc.dram_tensor("kvp", [nsrc, 64, 1024], F32, kind="ExternalInput").ap()
    io["q1_i"] = nc.dram_tensor("q1_i", [D, tl], BF16, kind="ExternalInput").ap()
    io["xn1_i"] = nc.dram_tensor("xn1_i", [D, tl], BF16, kind="ExternalInput").ap()
    io["x1_i"] = nc.dram_tensor("x1_i", [D, tl], F32, kind="ExternalInput").ap()
    io["ffn1_i"] = nc.dram_tensor("ffn1_i", [D, tl], F32, kind="ExternalInput").ap()
    io["Wo"] = nc.dram_tensor("Wo", [D, D], F32, kind="ExternalInput").ap()
    io["bo"] = nc.dram_tensor("bo", [D, 1], F32, kind="ExternalInput").ap()
    io["gamma"] = nc.dram_tensor("gamma", [D, 1], F32, kind="ExternalInput").ap()
    io["beta"] = nc.dram_tensor("beta", [D, 1], F32, kind="ExternalInput").ap()
    io["x2_o"] = nc.dram_tensor("x2_o", [D, tl], F32, kind="ExternalOutput").ap()
    with ExitStack() as st:
        cx = Ctx(nc, st)
        cx.sem("ld", "wla", "wc", "tf", "sv", "vv", "so", "sa", "svc", "td", "tb", "tc", "cc")
        emit_stageE(nc, cx, io, tl, nsrc)
    return nc


def emit_stageE(nc, cx, io, tl=TL, nsrc=8):
    (kvp, q1_i, xn1_i, x1_i, ffn1_i, Wo, bo, gamma, beta, x2_o) = (
        io["kvp"], io["q1_i"], io["xn1_i"], io["x1_i"], io["ffn1_i"], io["Wo"],
        io["bo"], io["gamma"], io["beta"], io["x2_o"])
    kv_summed = io.get("kv_summed", False)
    NC4 = tl // 512

    with cx.stage("E_"):
        st = cx.st
        consts = cx.sb("consts", [128, 4], F32)
        constd = nc.inline_tensor(np.tile(np.array([[1e-5, 1e-6, 1.0, 0.0]], np.float32), (128, 1)),
                                  name="constdE").ap()
        gb = cx.sb("gb", [128, 24], F32)
        ones1 = cx.sb("ones1E", [128, 1], F32, side="right")
        onesr = cx.sb("onesrE", [1, 128], F32, side="right")
        y2f = cx.sb("y2f", [128, 8 * tl], F32, side="right")
        attn_sb = cx.sb("attn_sb", [128, 8 * tl], BF16, side="right")

        with nc.Block() as b0:
            @b0.sync
            def _(sync):
                _n = nc.allow_non_contiguous_dma(reason="small vectors")
                _n.__enter__()
                cx.inc(sync.dma_start(out=consts[:, :], in_=constd), "ld", 16)
                cx.inc(sync.dma_start(out=gb[:, 0:8].rearrange("p (a o) -> p a o", o=1),
                                      in_=gamma.rearrange("(a p) o -> p a o", p=128)), "ld", 16)
                cx.inc(sync.dma_start(out=gb[:, 8:16].rearrange("p (a o) -> p a o", o=1),
                                      in_=beta.rearrange("(a p) o -> p a o", p=128)), "ld", 16)
                cx.inc(sync.dma_start(out=gb[:, 16:24].rearrange("p (a o) -> p a o", o=1),
                                      in_=bo.rearrange("(a p) o -> p a o", p=128)), "ld", 16)
                onesc = nc.inline_tensor(np.ones((1, 128), np.float32), name="onescE").ap()
                onescol = nc.inline_tensor(np.ones((128, 1), np.float32), name="onescolE").ap()
                cx.inc(sync.dma_start(out=onesr[:, :], in_=onesc), "ld", 16)
                cx.inc(sync.dma_start(out=ones1[:, :], in_=onescol), "ld", 16)
                _n.__exit__(None, None, None)

        with ExitStack() as ph:
            kvt = cx.sb("kvt", [64, nsrc * 1024], F32, ph)
            kvb = cx.sb("kvb", [64, 1024], BF16, ph)
            qh = cx.sb("qh", [64, 16 * tl], BF16, ph)
            p_at = [cx.ps(f"p_at{i}", [64, 512], ph) for i in range(4)]
            snap = cx.snapshot()
            with nc.Block() as b:
                @b.sync
                def _(sync):
                    cx.barrier_at(sync, snap)
                    if kv_summed:
                        cx.inc(sync.dma_start(out=kvt[:, 0:1024], in_=kvp), "wla", 16)
                    else:
                        for s in range(nsrc):
                            cx.inc(sync.dma_start(out=kvt[:, s * 1024:(s + 1) * 1024], in_=kvp[s]), "wla", 16)
                    for h in range(16):
                        cx.inc(sync.dma_start(out=qh[:, h * tl:(h + 1) * tl],
                                              in_=q1_i[h * 64:(h + 1) * 64, :]), "wla", 16)

                @b.vector
                def _(vector):
                    cx.barrier_at(vector, snap)
                    cx.wait(vector, "wla", snap["wla"] + 16 * (1 if kv_summed else nsrc))
                    if not kv_summed:
                        for s in range(4):
                            cx.inc(vector.tensor_tensor(kvt[:, s * 1024:(s + 1) * 1024],
                                                        kvt[:, s * 1024:(s + 1) * 1024],
                                                        kvt[:, (s + 4) * 1024:(s + 5) * 1024], ALU.add), "vv", 1)
                        for s in range(2):
                            cx.inc(vector.tensor_tensor(kvt[:, s * 1024:(s + 1) * 1024],
                                                        kvt[:, s * 1024:(s + 1) * 1024],
                                                        kvt[:, (s + 2) * 1024:(s + 3) * 1024], ALU.add), "vv", 1)
                        cx.inc(vector.tensor_tensor(kvt[:, 0:1024], kvt[:, 0:1024],
                                                    kvt[:, 1024:2048], ALU.add), "vv", 1)
                    cx.inc(vector.tensor_copy(kvb[:, :], kvt[:, 0:1024]), "wc", 1)

                @b.tensor
                def _(tensor):
                    cx.barrier_at(tensor, snap)
                    cx.wait(tensor, "wc", snap["wc"] + 1)
                    cx.wait(tensor, "wla", snap["wla"] + 16 * ((1 if kv_summed else nsrc) + 16))
                    for h in range(16):
                        if h >= 1:
                            cx.wait(tensor, "sv", snap["sv"] + h * 4)
                        inst = None
                        for c2 in range(4):
                            inst = tensor.matmul(p_at[c2][:, :], kvb[:, h * 64:(h + 1) * 64],
                                                 qh[:, h * tl + c2 * 512:h * tl + (c2 + 1) * 512],
                                                 start=True, stop=True)
                            cx.inc(inst, "tf", 1)

                @b.scalar
                def _(scalar):
                    cx.barrier_at(scalar, snap)
                    for h in range(16):
                        for c2 in range(4):
                            cx.wait(scalar, "tf", snap["tf"] + h * 4 + c2 + 1)
                            m, pb = h // 2, (h % 2) * 64
                            cx.inc(scalar.activation(
                                attn_sb[pb:pb + 64, m * tl + c2 * 512:m * tl + (c2 + 1) * 512],
                                p_at[c2][:, :], AF.Copy), "sv", 1)

        with ExitStack() as ph:
            xn1s = cx.sb("xn1s", [128, 8 * tl], BF16, ph)
            wfE = [cx.sb(f"wfE{i}", [128, 1024], F32, ph) for i in range(2)]
            wbE = [cx.sb(f"wbE{i}", [128, 1024], BF16, ph) for i in range(2)]
            ppE = [cx.ps(f"ppE{i}", [128, 512], ph) for i in range(8)]
            snap = cx.snapshot()
            with nc.Block() as b:
                @b.sync
                def _(sync):
                    cx.barrier_at(sync, snap)
                    cx.inc(sync.dma_start(out=xn1s[:, :].rearrange("p (a t) -> p a t", a=8),
                                          in_=xn1_i.rearrange("(a p) t -> p a t", p=128)), "wla", 16)
                    for m in range(8):
                        if m >= 2:
                            cx.wait(sync, "wc", snap["wc"] + m - 1)
                        cx.inc(sync.dma_start(out=wfE[m % 2][:, :].rearrange("p (a q) -> p a q", q=128),
                                              in_=Wo[:, m * 128:(m + 1) * 128]
                                              .rearrange("(a p) q -> p a q", p=128)), "wla", 16)

                @b.vector
                def _(vector):
                    cx.barrier_at(vector, snap)
                    for m in range(8):
                        cx.wait(vector, "wla", snap["wla"] + 16 * (m + 2))
                        if m >= 2:
                            cx.wait(vector, "tf", snap["tf"] + m - 1)
                        cx.inc(vector.tensor_copy(wbE[m % 2][:, :], wfE[m % 2][:, :]), "wc", 1)
                        cx.wait(vector, "sv", snap["sv"] + 4 * (m + 1))
                        sl = slice(m * tl, (m + 1) * tl)
                        cx.inc(vector.tensor_tensor(y2f[:, sl], y2f[:, sl], xn1s[:, sl], ALU.add), "vv", 1)

                @b.tensor
                def _(tensor):
                    cx.barrier_at(tensor, snap)
                    for m in range(8):
                        cx.wait(tensor, "wc", snap["wc"] + m + 1)
                        if m >= 2:
                            cx.wait(tensor, "sv", snap["sv"] + 4 * (m - 1))
                        inst = None
                        for k in range(8):
                            for c2 in range(4):
                                inst = tensor.matmul(ppE[(m % 2) * 4 + c2][:, :],
                                                     wbE[m % 2][:, k * 128:(k + 1) * 128],
                                                     attn_sb[:, k * tl + c2 * 512:k * tl + (c2 + 1) * 512],
                                                     start=(k == 0), stop=(k == 7))
                        cx.inc(inst, "tf", 1)

                @b.scalar
                def _(scalar):
                    cx.barrier_at(scalar, snap)
                    for m in range(8):
                        cx.wait(scalar, "tf", snap["tf"] + m + 1)
                        for c2 in range(4):
                            cx.inc(scalar.activation(y2f[:, m * tl + c2 * 512:m * tl + (c2 + 1) * 512],
                                                     ppE[(m % 2) * 4 + c2][:, :], AF.Identity,
                                                     bias=gb[:, 16 + m:16 + m + 1]), "sv", 1)

        with ExitStack() as ph:
            emit_ln_phase(nc, cx, ph, tl, y2f, None, None, ones1, onesr, consts[:, 1:2], rms=False)

        with ExitStack() as ph:
            x1s = [cx.sb(f"x1s{i}", [128, tl], F32, ph) for i in range(2)]
            f1s = [cx.sb(f"f1s{i}", [128, tl], F32, ph) for i in range(2)]
            outs = [cx.sb(f"outs{i}", [128, tl], F32, ph) for i in range(2)]
            snap = cx.snapshot()
            with nc.Block() as b:
                @b.sync
                def _(sync):
                    cx.barrier_at(sync, snap)
                    for m in range(8):
                        if m >= 2:
                            cx.wait(sync, "vv", snap["vv"] + 3 * (m - 1))
                        cx.inc(sync.dma_start(out=x1s[m % 2][:, :], in_=x1_i[m * 128:(m + 1) * 128, :]), "wla", 16)
                        cx.inc(sync.dma_start(out=f1s[m % 2][:, :], in_=ffn1_i[m * 128:(m + 1) * 128, :]), "wla", 16)
                        cx.wait(sync, "vv", snap["vv"] + 3 * (m + 1))
                        cx.inc(sync.dma_start(out=x2_o[m * 128:(m + 1) * 128, :], in_=outs[m % 2][:, :]), "so", 16)

                @b.vector
                def _(vector):
                    cx.barrier_at(vector, snap)
                    for m in range(8):
                        cx.wait(vector, "wla", snap["wla"] + 16 * (2 * m + 2))
                        if m >= 2:
                            cx.wait(vector, "so", snap["so"] + 16 * (m - 1))
                        sl = slice(m * tl, (m + 1) * tl)
                        cx.inc(vector.tensor_scalar(outs[m % 2][:, :], y2f[:, sl],
                                                    gb[:, m:m + 1], gb[:, 8 + m:8 + m + 1],
                                                    ALU.mult, ALU.add), "vv", 1)
                        cx.inc(vector.tensor_tensor(outs[m % 2][:, :], outs[m % 2][:, :],
                                                    x1s[m % 2][:, :], ALU.add), "vv", 1)
                        cx.inc(vector.tensor_tensor(outs[m % 2][:, :], outs[m % 2][:, :],
                                                    f1s[m % 2][:, :], ALU.add), "vv", 1)

        _finalize(nc, cx)



def kernel_unfused(**inputs):
    f32 = np.float32
    inp = {k: np.asarray(v) for k, v in inputs.items()}
    x = inp['L0_attn_x'][0]
    xT = np.ascontiguousarray(x.T)

    w1T = np.ascontiguousarray(inp['L0_ffn_w1_w'].T)
    w2T = np.ascontiguousarray(inp['L0_ffn_w2_w'].T)
    h1T = np.ascontiguousarray(inp['L0_ffn_h1_w'].T)
    h2T = np.ascontiguousarray(inp['L0_ffn_h2_w'].T)
    w3T = np.ascontiguousarray(inp['L0_ffn_w3_w'].T)
    w_in = inp['L0_attn_w_in']
    convw = inp['L0_attn_conv_w'][:, 0, :]
    wx = inp['L0_attn_w_x']
    wdt = inp['L0_attn_w_dt']
    gate = inp['L1_ffn_gate_w'].reshape(DH, D)
    up = inp['L1_ffn_up_w'].reshape(DH, D)
    down = inp['L1_ffn_down_w'].transpose(0, 2, 1).reshape(DH, D)

    cores = list(range(NCORES))

    ncA = build_stageA()
    mapsA = []
    for c in cores:
        mapsA.append({
            "xT": np.ascontiguousarray(xT[:, c * TL:(c + 1) * TL]),
            "normw": inp['L0_attn_norm_w'].reshape(-1, 1),
            "w1T": w1T, "w2T": w2T, "h1T": h1T, "h2T": h2T, "w3T": w3T,
            "w1s": inp['L0_ffn_w1_s'].reshape(-1, 1),
            "w2s": inp['L0_ffn_w2_s'].reshape(-1, 1),
            "h1s": inp['L0_ffn_h1_s'].reshape(-1, 1),
            "h2s": inp['L0_ffn_h2_s'].reshape(-1, 1),
            "w3s": inp['L0_ffn_w3_s'].reshape(-1, 1),
        })
    resA = run_bass_kernel_spmd(ncA, mapsA, core_ids=cores)
    xnr_full = np.concatenate([resA.results[c]['xnr_o'] for c in cores], axis=1)
    xn_parts = [resA.results[c]['xn_o'] for c in cores]
    ffn0_parts = [resA.results[c]['ffn0_o'] for c in cores]

    ncB = build_stageB()
    mapsB = []
    for c in cores:
        sl = slice(c * DSL, (c + 1) * DSL)
        mapsB.append({
            "xnr": xnr_full,
            "winT": np.ascontiguousarray(np.concatenate(
                [w_in[:, sl], w_in[:, DI + c * DSL:DI + (c + 1) * DSL]], axis=1)),
            "convw": np.ascontiguousarray(convw[sl]),
            "convb": np.ascontiguousarray(inp['L0_attn_conv_b'][sl].reshape(-1, 1)),
            "wx": np.ascontiguousarray(wx[sl]),
        })
    resB = run_bass_kernel_spmd(ncB, mapsB, core_ids=cores)
    xdbcp_all = np.stack([np.asarray(resB.results[c]['xdbcp_o']) for c in cores])

    ncC = build_stageC()
    mapsC = []
    for c in cores:
        sl = slice(c * DSL, (c + 1) * DSL)
        mapsC.append({
            "xp": xdbcp_all,
            "u_i": resB.results[c]['u_o'],
            "sres_i": resB.results[c]['sres_o'],
            "wdt": np.ascontiguousarray(wdt[:, sl]),
            "bdt": np.ascontiguousarray(inp['L0_attn_b_dt'][sl].reshape(-1, 1)),
            "alog": np.ascontiguousarray(inp['L0_attn_A_log'][sl]),
            "dsl": np.ascontiguousarray(inp['L0_attn_D'][sl].reshape(-1, 1)),
        })
    resC = run_bass_kernel_spmd(ncC, mapsC, core_ids=cores)
    y_full = np.concatenate([np.asarray(resC.results[c]['y_o']) for c in cores], axis=0)

    ncD = build_stageD()
    commonD = {
        "woutT": np.ascontiguousarray(inp['L0_attn_w_out']),
        "Wq": np.ascontiguousarray(inp['L1_attn_Wq']),
        "Wk": np.ascontiguousarray(inp['L1_attn_Wk']),
        "Wv": np.ascontiguousarray(inp['L1_attn_Wv']),
        "bq": inp['L1_attn_bq'].reshape(-1, 1),
        "bk": inp['L1_attn_bk'].reshape(-1, 1),
        "bv": inp['L1_attn_bv'].reshape(-1, 1),
        "routerT": np.ascontiguousarray(inp['L1_ffn_router_w'].T),
        "rbias": inp['L1_ffn_router_bias'].reshape(-1, 1),
        "gateT": np.ascontiguousarray(gate.T),
        "upT": np.ascontiguousarray(up.T),
        "downT": np.ascontiguousarray(down),
        "sgT": np.ascontiguousarray(inp['L1_ffn_sgate_w'].T),
        "suT": np.ascontiguousarray(inp['L1_ffn_sup_w'].T),
        "sdT": np.ascontiguousarray(inp['L1_ffn_sdown_w'].T),
    }
    mapsD = []
    for c in cores:
        m = dict(commonD)
        m.update({
            "y_i": np.ascontiguousarray(y_full[:, c * TL:(c + 1) * TL]),
            "xT": np.ascontiguousarray(xT[:, c * TL:(c + 1) * TL]),
            "xn_i": xn_parts[c],
            "ffn0_i": ffn0_parts[c],
        })
        mapsD.append(m)
    resD = run_bass_kernel_spmd(ncD, mapsD, core_ids=cores)
    kvp_all = np.stack([np.asarray(resD.results[c]['kvp_o']) for c in cores])

    ncE = build_stageE()
    mapsE = []
    for c in cores:
        mapsE.append({
            "kvp": kvp_all,
            "q1_i": resD.results[c]['q1_o'],
            "xn1_i": resD.results[c]['xn1_o'],
            "x1_i": resD.results[c]['x1_o'],
            "ffn1_i": resD.results[c]['ffn1_o'],
            "Wo": np.ascontiguousarray(inp['L1_attn_Wo']),
            "bo": inp['L1_attn_bo'].reshape(-1, 1),
            "gamma": inp['L1_attn_gamma'].reshape(-1, 1),
            "beta": inp['L1_attn_beta'].reshape(-1, 1),
        })
    resE = run_bass_kernel_spmd(ncE, mapsE, core_ids=cores)
    x2 = np.concatenate([np.asarray(resE.results[c]['x2_o']) for c in cores], axis=1)
    return np.ascontiguousarray(x2.T).reshape(1, T_FULL, D).astype(np.float32)



def _emit_collective(nc, cx, kind, op, ins, outs):
    snap = cx.snapshot()
    with nc.Block() as b:
        @b.gpsimd
        def _(g):
            cx.barrier_at(g, snap)
            cx.inc(g.collective_compute(kind, op, replica_groups=[list(range(NCORES))],
                                        ins=ins, outs=outs), "cc", 1)


def build_stageDE(tl=TL):
    nc = bass.Bass("TRN2", target_bir_lowering=False, num_devices=NCORES)
    ext = {}
    def ein(nm, shape, dt=F32):
        ext[nm] = nc.dram_tensor(nm, shape, dt, kind="ExternalInput").ap()
    ein("y_i", [DI, tl], BF16); ein("xT", [D, tl]); ein("xn_i", [D, tl], BF16)
    ein("ffn0_i", [D, tl])
    ein("woutT", [DI, D]); ein("Wq", [D, D]); ein("Wk", [D, D]); ein("Wv", [D, D])
    ein("bq", [D, 1]); ein("bk", [D, 1]); ein("bv", [D, 1])
    ein("routerT", [D, 32]); ein("rbias", [32, 1])
    ein("gateT", [D, DH]); ein("upT", [D, DH]); ein("downT", [DH, D])
    ein("sgT", [D, 512]); ein("suT", [D, 512]); ein("sdT", [512, D])
    ein("Wo", [D, D]); ein("bo", [D, 1]); ein("gamma", [D, 1]); ein("beta", [D, 1])
    x2_o = nc.dram_tensor("x2_o", [D, tl], F32, kind="ExternalOutput").ap()

    x1_loc = nc.dram_tensor("f_x1_loc", [D, tl], F32).ap()
    xn1_loc = nc.dram_tensor("f_xn1_loc", [D, tl], BF16).ap()
    q1_loc = nc.dram_tensor("f_q1_loc", [D, tl], BF16).ap()
    kvp_loc = nc.dram_tensor("f_kvp_loc", [64, 1024], F32).ap()
    kv_sum = nc.dram_tensor("f_kv_sum", [64, 1024], F32).ap()
    ffn1_loc = nc.dram_tensor("f_ffn1_loc", [D, tl], F32).ap()

    with ExitStack() as st:
        cx = Ctx(nc, st)
        cx.sem("ld", "wla", "wc", "tf", "sv", "vv", "so", "sa", "svc", "td", "tb", "tc", "cc")
        ioD = {"y_i": ext["y_i"], "xT": ext["xT"], "xn_i": ext["xn_i"],
               "ffn0_i": ext["ffn0_i"]}
        for nm in ["woutT", "Wq", "Wk", "Wv", "bq", "bk", "bv", "routerT", "rbias",
                   "gateT", "upT", "downT", "sgT", "suT", "sdT"]:
            ioD[nm] = ext[nm]
        ioD.update({"x1_o": x1_loc, "xn1_o": xn1_loc, "q1_o": q1_loc,
                    "kvp_o": kvp_loc, "ffn1_o": ffn1_loc})
        emit_stageD(nc, cx, ioD, tl)

        _emit_collective(nc, cx, "AllReduce", ALU.add, [kvp_loc], [kv_sum])

        ioE = {"kvp": kv_sum, "kv_summed": True, "q1_i": q1_loc, "xn1_i": xn1_loc,
               "x1_i": x1_loc, "ffn1_i": ffn1_loc, "Wo": ext["Wo"], "bo": ext["bo"],
               "gamma": ext["gamma"], "beta": ext["beta"], "x2_o": x2_o}
        emit_stageE(nc, cx, ioE, tl)
    return nc


def build_stageABC():
    nc = bass.Bass("TRN2", target_bir_lowering=False, num_devices=NCORES)
    tl = TL
    ext = {}
    def ein(nm, shape, dt=F32):
        ext[nm] = nc.dram_tensor(nm, shape, dt, kind="ExternalInput").ap()
    ein("xT", [D, tl]); ein("normw", [D, 1])
    for nm, k, m in [("w1T", D, DH), ("w2T", D, DH), ("h1T", DH, DH), ("h2T", DH, DH), ("w3T", DH, D)]:
        ein(nm, [k, m])
    for nm, d_ in [("w1s", DH), ("w2s", DH), ("h1s", DH), ("h2s", DH), ("w3s", D)]:
        ein(nm, [d_, 1])
    ein("winT", [D, 512]); ein("convw", [256, 4]); ein("convb", [256, 1]); ein("wx", [256, 96])
    ein("wdt", [DTR, 256]); ein("bdt", [256, 1]); ein("alog", [256, NST]); ein("dsl", [256, 1])
    xn_o = nc.dram_tensor("xn_o", [D, tl], BF16, kind="ExternalOutput").ap()
    ffn0_o = nc.dram_tensor("ffn0_o", [D, tl], F32, kind="ExternalOutput").ap()
    y_o = nc.dram_tensor("y_o", [256, T_FULL], BF16, kind="ExternalOutput").ap()

    xnr_loc = nc.dram_tensor("f_xnr_loc", [D, tl], BF16).ap()
    xnr_all = nc.dram_tensor("f_xnr_all", [NCORES * D, tl], BF16).ap()
    u_loc = nc.dram_tensor("f_u_loc", [256, T_FULL], BF16).ap()
    sres_loc = nc.dram_tensor("f_sres_loc", [256, T_FULL], BF16).ap()
    xdbcp_loc = nc.dram_tensor("f_xdbcp_loc", [96, T_FULL], F32).ap()
    xdbc_sum = nc.dram_tensor("f_xdbc_sum", [96, T_FULL], F32).ap()

    with ExitStack() as st:
        cx = Ctx(nc, st)
        cx.sem("ld", "wla", "wc", "tf", "sv", "vv", "so", "sa", "svc", "td", "tb", "tc", "cc")
        ioA = {nm: ext[nm] for nm in ["xT", "normw", "w1T", "w2T", "h1T", "h2T", "w3T",
                                      "w1s", "w2s", "h1s", "h2s", "w3s"]}
        ioA.update({"xnr_o": xnr_loc, "xn_o": xn_o, "ffn0_o": ffn0_o})
        emit_stageA(nc, cx, ioA, tl)

        _emit_collective(nc, cx, "AllGather", ALU.bypass, [xnr_loc], [xnr_all])

        ioB = {"xnr": xnr_all, "winT": ext["winT"], "convw": ext["convw"],
               "convb": ext["convb"], "wx": ext["wx"],
               "u_o": u_loc, "sres_o": sres_loc, "xdbcp_o": xdbcp_loc}
        emit_stageB(nc, cx, ioB, T_FULL, xnr_gathered=True)

        _emit_collective(nc, cx, "AllReduce", ALU.add, [xdbcp_loc], [xdbc_sum])

        ioC = {"xp": xdbc_sum, "xp_summed": True, "u_i": u_loc, "sres_i": sres_loc,
               "wdt": ext["wdt"], "bdt": ext["bdt"], "alog": ext["alog"], "dsl": ext["dsl"],
               "y_o": y_o}
        emit_stageC(nc, cx, ioC, T_FULL)
    return nc


def kernel_v3(**inputs):
    inp = {k: np.asarray(v) for k, v in inputs.items()}
    x = inp['L0_attn_x'][0]
    xT = np.ascontiguousarray(x.T)
    w_in = inp['L0_attn_w_in']
    convw = inp['L0_attn_conv_w'][:, 0, :]
    wdt = inp['L0_attn_w_dt']
    gate = inp['L1_ffn_gate_w'].reshape(DH, D)
    up = inp['L1_ffn_up_w'].reshape(DH, D)
    down = inp['L1_ffn_down_w'].transpose(0, 2, 1).reshape(DH, D)
    cores = list(range(NCORES))

    ncABC = build_stageABC()
    commonA = {
        "normw": inp['L0_attn_norm_w'].reshape(-1, 1),
        "w1T": np.ascontiguousarray(inp['L0_ffn_w1_w'].T),
        "w2T": np.ascontiguousarray(inp['L0_ffn_w2_w'].T),
        "h1T": np.ascontiguousarray(inp['L0_ffn_h1_w'].T),
        "h2T": np.ascontiguousarray(inp['L0_ffn_h2_w'].T),
        "w3T": np.ascontiguousarray(inp['L0_ffn_w3_w'].T),
        "w1s": inp['L0_ffn_w1_s'].reshape(-1, 1),
        "w2s": inp['L0_ffn_w2_s'].reshape(-1, 1),
        "h1s": inp['L0_ffn_h1_s'].reshape(-1, 1),
        "h2s": inp['L0_ffn_h2_s'].reshape(-1, 1),
        "w3s": inp['L0_ffn_w3_s'].reshape(-1, 1),
    }
    mapsABC = []
    for c in cores:
        sl = slice(c * DSL, (c + 1) * DSL)
        m = dict(commonA)
        m.update({
            "xT": np.ascontiguousarray(xT[:, c * TL:(c + 1) * TL]),
            "winT": np.ascontiguousarray(np.concatenate(
                [w_in[:, sl], w_in[:, DI + c * DSL:DI + (c + 1) * DSL]], axis=1)),
            "convw": np.ascontiguousarray(convw[sl]),
            "convb": np.ascontiguousarray(inp['L0_attn_conv_b'][sl].reshape(-1, 1)),
            "wx": np.ascontiguousarray(inp['L0_attn_w_x'][sl]),
            "wdt": np.ascontiguousarray(wdt[:, sl]),
            "bdt": np.ascontiguousarray(inp['L0_attn_b_dt'][sl].reshape(-1, 1)),
            "alog": np.ascontiguousarray(inp['L0_attn_A_log'][sl]),
            "dsl": np.ascontiguousarray(inp['L0_attn_D'][sl].reshape(-1, 1)),
        })
        mapsABC.append(m)
    resABC = run_bass_kernel_spmd(ncABC, mapsABC, core_ids=cores)
    xn_parts = [resABC.results[c]['xn_o'] for c in cores]
    ffn0_parts = [resABC.results[c]['ffn0_o'] for c in cores]
    y_full = np.concatenate([np.asarray(resABC.results[c]['y_o']) for c in cores], axis=0)

    ncDE = build_stageDE()
    commonDE = {
        "woutT": np.ascontiguousarray(inp['L0_attn_w_out']),
        "Wq": np.ascontiguousarray(inp['L1_attn_Wq']),
        "Wk": np.ascontiguousarray(inp['L1_attn_Wk']),
        "Wv": np.ascontiguousarray(inp['L1_attn_Wv']),
        "bq": inp['L1_attn_bq'].reshape(-1, 1),
        "bk": inp['L1_attn_bk'].reshape(-1, 1),
        "bv": inp['L1_attn_bv'].reshape(-1, 1),
        "routerT": np.ascontiguousarray(inp['L1_ffn_router_w'].T),
        "rbias": inp['L1_ffn_router_bias'].reshape(-1, 1),
        "gateT": np.ascontiguousarray(gate.T),
        "upT": np.ascontiguousarray(up.T),
        "downT": np.ascontiguousarray(down),
        "sgT": np.ascontiguousarray(inp['L1_ffn_sgate_w'].T),
        "suT": np.ascontiguousarray(inp['L1_ffn_sup_w'].T),
        "sdT": np.ascontiguousarray(inp['L1_ffn_sdown_w'].T),
        "Wo": np.ascontiguousarray(inp['L1_attn_Wo']),
        "bo": inp['L1_attn_bo'].reshape(-1, 1),
        "gamma": inp['L1_attn_gamma'].reshape(-1, 1),
        "beta": inp['L1_attn_beta'].reshape(-1, 1),
    }
    mapsDE = []
    for c in cores:
        m = dict(commonDE)
        m.update({
            "y_i": np.ascontiguousarray(y_full[:, c * TL:(c + 1) * TL]),
            "xT": np.ascontiguousarray(xT[:, c * TL:(c + 1) * TL]),
            "xn_i": xn_parts[c],
            "ffn0_i": ffn0_parts[c],
        })
        mapsDE.append(m)
    resDE = run_bass_kernel_spmd(ncDE, mapsDE, core_ids=cores)
    x2 = np.concatenate([np.asarray(resDE.results[c]['x2_o']) for c in cores], axis=1)
    return np.ascontiguousarray(x2.T).reshape(1, T_FULL, D).astype(np.float32)


def kernel_v2(**inputs):
    inp = {k: np.asarray(v) for k, v in inputs.items()}
    x = inp['L0_attn_x'][0]
    xT = np.ascontiguousarray(x.T)
    w_in = inp['L0_attn_w_in']
    convw = inp['L0_attn_conv_w'][:, 0, :]
    wdt = inp['L0_attn_w_dt']
    gate = inp['L1_ffn_gate_w'].reshape(DH, D)
    up = inp['L1_ffn_up_w'].reshape(DH, D)
    down = inp['L1_ffn_down_w'].transpose(0, 2, 1).reshape(DH, D)
    cores = list(range(NCORES))

    ncA = build_stageA()
    mapsA = []
    for c in cores:
        mapsA.append({
            "xT": np.ascontiguousarray(xT[:, c * TL:(c + 1) * TL]),
            "normw": inp['L0_attn_norm_w'].reshape(-1, 1),
            "w1T": np.ascontiguousarray(inp['L0_ffn_w1_w'].T),
            "w2T": np.ascontiguousarray(inp['L0_ffn_w2_w'].T),
            "h1T": np.ascontiguousarray(inp['L0_ffn_h1_w'].T),
            "h2T": np.ascontiguousarray(inp['L0_ffn_h2_w'].T),
            "w3T": np.ascontiguousarray(inp['L0_ffn_w3_w'].T),
            "w1s": inp['L0_ffn_w1_s'].reshape(-1, 1),
            "w2s": inp['L0_ffn_w2_s'].reshape(-1, 1),
            "h1s": inp['L0_ffn_h1_s'].reshape(-1, 1),
            "h2s": inp['L0_ffn_h2_s'].reshape(-1, 1),
            "w3s": inp['L0_ffn_w3_s'].reshape(-1, 1),
        })
    resA = run_bass_kernel_spmd(ncA, mapsA, core_ids=cores)
    xnr_full = np.concatenate([resA.results[c]['xnr_o'] for c in cores], axis=1)
    xn_parts = [resA.results[c]['xn_o'] for c in cores]
    ffn0_parts = [resA.results[c]['ffn0_o'] for c in cores]

    ncB = build_stageB()
    mapsB = []
    for c in cores:
        sl = slice(c * DSL, (c + 1) * DSL)
        mapsB.append({
            "xnr": xnr_full,
            "winT": np.ascontiguousarray(np.concatenate(
                [w_in[:, sl], w_in[:, DI + c * DSL:DI + (c + 1) * DSL]], axis=1)),
            "convw": np.ascontiguousarray(convw[sl]),
            "convb": np.ascontiguousarray(inp['L0_attn_conv_b'][sl].reshape(-1, 1)),
            "wx": np.ascontiguousarray(inp['L0_attn_w_x'][sl]),
        })
    resB = run_bass_kernel_spmd(ncB, mapsB, core_ids=cores)
    xdbcp_all = np.stack([np.asarray(resB.results[c]['xdbcp_o']) for c in cores])

    ncC = build_stageC()
    mapsC = []
    for c in cores:
        sl = slice(c * DSL, (c + 1) * DSL)
        mapsC.append({
            "xp": xdbcp_all,
            "u_i": resB.results[c]['u_o'],
            "sres_i": resB.results[c]['sres_o'],
            "wdt": np.ascontiguousarray(wdt[:, sl]),
            "bdt": np.ascontiguousarray(inp['L0_attn_b_dt'][sl].reshape(-1, 1)),
            "alog": np.ascontiguousarray(inp['L0_attn_A_log'][sl]),
            "dsl": np.ascontiguousarray(inp['L0_attn_D'][sl].reshape(-1, 1)),
        })
    resC = run_bass_kernel_spmd(ncC, mapsC, core_ids=cores)
    y_full = np.concatenate([np.asarray(resC.results[c]['y_o']) for c in cores], axis=0)

    ncDE = build_stageDE()
    commonDE = {
        "woutT": np.ascontiguousarray(inp['L0_attn_w_out']),
        "Wq": np.ascontiguousarray(inp['L1_attn_Wq']),
        "Wk": np.ascontiguousarray(inp['L1_attn_Wk']),
        "Wv": np.ascontiguousarray(inp['L1_attn_Wv']),
        "bq": inp['L1_attn_bq'].reshape(-1, 1),
        "bk": inp['L1_attn_bk'].reshape(-1, 1),
        "bv": inp['L1_attn_bv'].reshape(-1, 1),
        "routerT": np.ascontiguousarray(inp['L1_ffn_router_w'].T),
        "rbias": inp['L1_ffn_router_bias'].reshape(-1, 1),
        "gateT": np.ascontiguousarray(gate.T),
        "upT": np.ascontiguousarray(up.T),
        "downT": np.ascontiguousarray(down),
        "sgT": np.ascontiguousarray(inp['L1_ffn_sgate_w'].T),
        "suT": np.ascontiguousarray(inp['L1_ffn_sup_w'].T),
        "sdT": np.ascontiguousarray(inp['L1_ffn_sdown_w'].T),
        "Wo": np.ascontiguousarray(inp['L1_attn_Wo']),
        "bo": inp['L1_attn_bo'].reshape(-1, 1),
        "gamma": inp['L1_attn_gamma'].reshape(-1, 1),
        "beta": inp['L1_attn_beta'].reshape(-1, 1),
    }
    mapsDE = []
    for c in cores:
        m = dict(commonDE)
        m.update({
            "y_i": np.ascontiguousarray(y_full[:, c * TL:(c + 1) * TL]),
            "xT": np.ascontiguousarray(xT[:, c * TL:(c + 1) * TL]),
            "xn_i": xn_parts[c],
            "ffn0_i": ffn0_parts[c],
        })
        mapsDE.append(m)
    resDE = run_bass_kernel_spmd(ncDE, mapsDE, core_ids=cores)
    x2 = np.concatenate([np.asarray(resDE.results[c]['x2_o']) for c in cores], axis=1)
    return np.ascontiguousarray(x2.T).reshape(1, T_FULL, D).astype(np.float32)


def build_fused():
    nc = bass.Bass("TRN2", target_bir_lowering=False, num_devices=NCORES)
    tl = TL
    ext = {}
    def ein(nm, shape, dt=F32):
        ext[nm] = nc.dram_tensor(nm, shape, dt, kind="ExternalInput").ap()
    ein("xT", [D, tl]); ein("normw", [D, 1])
    for nm, k, m in [("w1T", D, DH), ("w2T", D, DH), ("h1T", DH, DH), ("h2T", DH, DH), ("w3T", DH, D)]:
        ein(nm, [k, m])
    for nm, d_ in [("w1s", DH), ("w2s", DH), ("h1s", DH), ("h2s", DH), ("w3s", D)]:
        ein(nm, [d_, 1])
    ein("winT", [D, 512]); ein("convw", [256, 4]); ein("convb", [256, 1]); ein("wx", [256, 96])
    ein("wdt", [DTR, 256]); ein("bdt", [256, 1]); ein("alog", [256, NST]); ein("dsl", [256, 1])
    ein("woutT", [DI, D]); ein("Wq", [D, D]); ein("Wk", [D, D]); ein("Wv", [D, D])
    ein("bq", [D, 1]); ein("bk", [D, 1]); ein("bv", [D, 1])
    ein("routerT", [D, 32]); ein("rbias", [32, 1])
    ein("gateT", [D, DH]); ein("upT", [D, DH]); ein("downT", [DH, D])
    ein("sgT", [D, 512]); ein("suT", [D, 512]); ein("sdT", [512, D])
    ein("Wo", [D, D]); ein("bo", [D, 1]); ein("gamma", [D, 1]); ein("beta", [D, 1])
    x2_o = nc.dram_tensor("x2_o", [D, tl], F32, kind="ExternalOutput").ap()

    xnr_loc = nc.dram_tensor("f_xnr_loc", [D, tl], BF16).ap()
    xnr_all = nc.dram_tensor("f_xnr_all", [NCORES * D, tl], BF16).ap()
    xn_loc = nc.dram_tensor("f_xn_loc", [D, tl], BF16).ap()
    ffn0_loc = nc.dram_tensor("f_ffn0_loc", [D, tl], F32).ap()
    u_loc = nc.dram_tensor("f_u_loc", [256, T_FULL], BF16).ap()
    sres_loc = nc.dram_tensor("f_sres_loc", [256, T_FULL], BF16).ap()
    xdbcp_loc = nc.dram_tensor("f_xdbcp_loc", [96, T_FULL], F32).ap()
    xdbc_sum = nc.dram_tensor("f_xdbc_sum", [96, T_FULL], F32).ap()
    y_loc = nc.dram_tensor("f_y_loc", [256, T_FULL], BF16).ap()
    y_all = nc.dram_tensor("f_y_all", [NCORES * 256, T_FULL], BF16).ap()
    x1_loc = nc.dram_tensor("f_x1_loc", [D, tl], F32).ap()
    xn1_loc = nc.dram_tensor("f_xn1_loc", [D, tl], BF16).ap()
    q1_loc = nc.dram_tensor("f_q1_loc", [D, tl], BF16).ap()
    kvp_loc = nc.dram_tensor("f_kvp_loc", [64, 1024], F32).ap()
    kv_sum = nc.dram_tensor("f_kv_sum", [64, 1024], F32).ap()
    ffn1_loc = nc.dram_tensor("f_ffn1_loc", [D, tl], F32).ap()

    with ExitStack() as st:
        cx = Ctx(nc, st)
        cx.sem("ld", "wla", "wc", "tf", "sv", "vv", "so", "sa", "svc", "td", "tb", "tc", "cc")

        ioA = {nm: ext[nm] for nm in ["xT", "normw", "w1T", "w2T", "h1T", "h2T", "w3T",
                                      "w1s", "w2s", "h1s", "h2s", "w3s"]}
        ioA.update({"xnr_o": xnr_loc, "xn_o": xn_loc, "ffn0_o": ffn0_loc})
        emit_stageA(nc, cx, ioA, tl)

        _emit_collective(nc, cx, "AllGather", ALU.bypass, [xnr_loc], [xnr_all])

        ioB = {"xnr": xnr_all, "winT": ext["winT"], "convw": ext["convw"],
               "convb": ext["convb"], "wx": ext["wx"],
               "u_o": u_loc, "sres_o": sres_loc, "xdbcp_o": xdbcp_loc}
        emit_stageB(nc, cx, ioB, T_FULL, xnr_gathered=True)

        _emit_collective(nc, cx, "AllReduce", ALU.add, [xdbcp_loc], [xdbc_sum])

        ioC = {"xp": xdbc_sum, "xp_summed": True, "u_i": u_loc, "sres_i": sres_loc,
               "wdt": ext["wdt"], "bdt": ext["bdt"], "alog": ext["alog"], "dsl": ext["dsl"],
               "y_o": y_loc}
        emit_stageC(nc, cx, ioC, T_FULL)

        _emit_collective(nc, cx, "AllGather", ALU.bypass, [y_loc], [y_all])

        ioD = {"y_i": y_all, "y_sliced_by_core": True, "xT": ext["xT"], "xn_i": xn_loc,
               "ffn0_i": ffn0_loc}
        for nm in ["woutT", "Wq", "Wk", "Wv", "bq", "bk", "bv", "routerT", "rbias",
                   "gateT", "upT", "downT", "sgT", "suT", "sdT"]:
            ioD[nm] = ext[nm]
        ioD.update({"x1_o": x1_loc, "xn1_o": xn1_loc, "q1_o": q1_loc,
                    "kvp_o": kvp_loc, "ffn1_o": ffn1_loc})
        emit_stageD(nc, cx, ioD, tl)

        _emit_collective(nc, cx, "AllReduce", ALU.add, [kvp_loc], [kv_sum])

        ioE = {"kvp": kv_sum, "kv_summed": True, "q1_i": q1_loc, "xn1_i": xn1_loc,
               "x1_i": x1_loc, "ffn1_i": ffn1_loc, "Wo": ext["Wo"], "bo": ext["bo"],
               "gamma": ext["gamma"], "beta": ext["beta"], "x2_o": x2_o}
        emit_stageE(nc, cx, ioE, tl)

    return nc


def kernel_fused(**inputs):
    inp = {k: np.asarray(v) for k, v in inputs.items()}
    x = inp['L0_attn_x'][0]
    xT = np.ascontiguousarray(x.T)
    w_in = inp['L0_attn_w_in']
    convw = inp['L0_attn_conv_w'][:, 0, :]
    gate = inp['L1_ffn_gate_w'].reshape(DH, D)
    up = inp['L1_ffn_up_w'].reshape(DH, D)
    down = inp['L1_ffn_down_w'].transpose(0, 2, 1).reshape(DH, D)

    common = {
        "normw": inp['L0_attn_norm_w'].reshape(-1, 1),
        "w1T": np.ascontiguousarray(inp['L0_ffn_w1_w'].T),
        "w2T": np.ascontiguousarray(inp['L0_ffn_w2_w'].T),
        "h1T": np.ascontiguousarray(inp['L0_ffn_h1_w'].T),
        "h2T": np.ascontiguousarray(inp['L0_ffn_h2_w'].T),
        "w3T": np.ascontiguousarray(inp['L0_ffn_w3_w'].T),
        "w1s": inp['L0_ffn_w1_s'].reshape(-1, 1),
        "w2s": inp['L0_ffn_w2_s'].reshape(-1, 1),
        "h1s": inp['L0_ffn_h1_s'].reshape(-1, 1),
        "h2s": inp['L0_ffn_h2_s'].reshape(-1, 1),
        "w3s": inp['L0_ffn_w3_s'].reshape(-1, 1),
        "woutT": np.ascontiguousarray(inp['L0_attn_w_out']),
        "Wq": np.ascontiguousarray(inp['L1_attn_Wq']),
        "Wk": np.ascontiguousarray(inp['L1_attn_Wk']),
        "Wv": np.ascontiguousarray(inp['L1_attn_Wv']),
        "bq": inp['L1_attn_bq'].reshape(-1, 1),
        "bk": inp['L1_attn_bk'].reshape(-1, 1),
        "bv": inp['L1_attn_bv'].reshape(-1, 1),
        "routerT": np.ascontiguousarray(inp['L1_ffn_router_w'].T),
        "rbias": inp['L1_ffn_router_bias'].reshape(-1, 1),
        "gateT": np.ascontiguousarray(gate.T),
        "upT": np.ascontiguousarray(up.T),
        "downT": np.ascontiguousarray(down),
        "sgT": np.ascontiguousarray(inp['L1_ffn_sgate_w'].T),
        "suT": np.ascontiguousarray(inp['L1_ffn_sup_w'].T),
        "sdT": np.ascontiguousarray(inp['L1_ffn_sdown_w'].T),
        "Wo": np.ascontiguousarray(inp['L1_attn_Wo']),
        "bo": inp['L1_attn_bo'].reshape(-1, 1),
        "gamma": inp['L1_attn_gamma'].reshape(-1, 1),
        "beta": inp['L1_attn_beta'].reshape(-1, 1),
    }
    nc = build_fused()
    maps = []
    for c in range(NCORES):
        sl = slice(c * DSL, (c + 1) * DSL)
        m = dict(common)
        m.update({
            "xT": np.ascontiguousarray(xT[:, c * TL:(c + 1) * TL]),
            "winT": np.ascontiguousarray(np.concatenate(
                [w_in[:, sl], w_in[:, DI + c * DSL:DI + (c + 1) * DSL]], axis=1)),
            "convw": np.ascontiguousarray(convw[sl]),
            "convb": np.ascontiguousarray(inp['L0_attn_conv_b'][sl].reshape(-1, 1)),
            "wx": np.ascontiguousarray(inp['L0_attn_w_x'][sl]),
            "wdt": np.ascontiguousarray(inp['L0_attn_w_dt'][:, sl]),
            "bdt": np.ascontiguousarray(inp['L0_attn_b_dt'][sl].reshape(-1, 1)),
            "alog": np.ascontiguousarray(inp['L0_attn_A_log'][sl]),
            "dsl": np.ascontiguousarray(inp['L0_attn_D'][sl].reshape(-1, 1)),
        })
        maps.append(m)
    res = run_bass_kernel_spmd(nc, maps, core_ids=list(range(NCORES)))
    x2 = np.concatenate([np.asarray(res.results[c]['x2_o']) for c in range(NCORES)], axis=1)
    return np.ascontiguousarray(x2.T).reshape(1, T_FULL, D).astype(np.float32)


def kernel(**inputs):
    return kernel_fused(**inputs)
